# version 04
# baseline (speedup 1.0000x reference)
import math
from contextlib import ExitStack

import numpy as np
import ml_dtypes

import concourse.bass as bass
import concourse.mybir as mybir
from concourse.bass_utils import run_bass_kernel_spmd

F32 = mybir.dt.float32
BF16 = mybir.dt.bfloat16
U8 = mybir.dt.uint8
AF = mybir.ActivationFunctionType
ALU = mybir.AluOpType
AX = mybir.AxisListType

S = 8192
D = 2048
TPC = 1024
NCORE = 8
H = 16
DK = 128
HPC = 2
DT = 16
FF = 4096
FT = 32
CH = 512
NCH = 2
INV_SQRT_DK = 1.0 / math.sqrt(DK)
LN_EPS = 1e-5
RMS_EPS = float(np.finfo(np.float32).eps)

TWO_PI = 2.0 * math.pi
CW_HI = 6.28125
CW_MID = float(np.float32(TWO_PI - CW_HI))
CW_LO = float(np.float32(TWO_PI - CW_HI - float(np.float32(TWO_PI - CW_HI))))
RND_C = float(np.float32(2.0 ** 23))
HALF_PI = float(np.float32(math.pi / 2.0))

DEBUG_TAP = False
ATTN_UPTO = "all"
BUILD_UPTO = "ffn2"
_PHASE_ORDER = ["tables", "ffn1", "proj", "attn", "wo", "ffn2"]

ENGS = ("te", "se", "ve", "sq", "gq", "cc")


def _ph_on(name):
    return (_PHASE_ORDER.index(name) <= _PHASE_ORDER.index(BUILD_UPTO))


class Ctx:

    def __init__(self, nc, eng_name, eng, sems, tn):
        self.nc = nc
        self.eng_name = eng_name
        self.eng = eng
        self.sems = sems
        self.tn = tn
        self.counts = {k: 0 for k in sems}
        self.last_wait = {k: 0 for k in sems}

    def on(self, name):
        return self.eng_name == name

    def _waits(self, waits):
        for s, v in waits:
            if v is None or v <= 0:
                continue
            if v > self.counts[s]:
                raise RuntimeError(f"wait on future {s}: {v} > {self.counts[s]}")
            if v > self.last_wait[s]:
                self.eng.wait_ge(self.sems[s], v)
                self.last_wait[s] = v

    def bump(self, sem, n):
        self.counts[sem] += n
        return self.counts[sem]

    def barrier(self):
        return [(s, self.counts[s]) for s in ("te", "se", "ve", "sq", "gq", "cc")]

    def dma(self, queue, out, in_, waits=(), **kw):
        sem = "sq" if queue == "sync" else "gq"
        if self.on("sync" if queue == "sync" else "gpsimd"):
            self._waits(waits)
            self.eng.dma_start(out=out, in_=in_, **kw).then_inc(self.sems[sem], 16)
        return self.bump(sem, 16)

    def mm(self, out, lhsT, rhs, start=True, stop=True, waits=(), inc=True):
        if self.on("tensor"):
            self._waits(waits)
            i = self.eng.matmul(out, lhsT, rhs, start=start, stop=stop)
            if inc:
                i.then_inc(self.sems["te"], 1)
        return self.bump("te", 1) if inc else None

    def transpose(self, out, in_, waits=(), inc=True):
        if self.on("tensor"):
            self._waits(waits)
            i = self.eng.transpose(out, in_, self.tn["eye_sb"][:, :])
            if inc:
                i.then_inc(self.sems["te"], 1)
        return self.bump("te", 1) if inc else None

    def act(self, out, in_, func, bias=0.0, scale=1.0, accum_out=None, waits=(),
            inc=True):
        if self.on("scalar"):
            self._waits(waits)
            i = self.eng.activation(out, in_, func, bias=bias, scale=scale,
                                    accum_out=accum_out)
            if inc:
                i.then_inc(self.sems["se"], 1)
        return self.bump("se", 1) if inc else None

    def vop(self, fn, *args, waits=(), inc=True, **kw):
        if self.on("vector"):
            self._waits(waits)
            i = getattr(self.eng, fn)(*args, **kw)
            if inc:
                i.then_inc(self.sems["ve"], 1)
        return self.bump("ve", 1) if inc else None

    def coll(self, kind, ins, outs, waits=()):
        if self.on("gpsimd"):
            self._waits(waits)
            self.eng.collective_compute(
                kind, ALU.bypass, replica_groups=[list(range(NCORE))],
                ins=ins, outs=outs,
            ).then_inc(self.sems["cc"])
        return self.bump("cc", 1)

    def gdma_off(self, out, dram_tensor, meta_idx, pattern, waits=()):
        if self.on("gpsimd"):
            self._waits(waits)
            self.eng.reg_load(self.roff, self.tn["meta_sb"][0:1,
                                                            meta_idx:meta_idx + 1])
            src = bass.AP(dram_tensor, self.roff, pattern)
            self.eng.dma_start(out=out, in_=src).then_inc(self.sems["gq"], 16)
        return self.bump("gq", 16)

    def gwait(self, eng_name, waits):
        if self.on(eng_name):
            self._waits(waits)

    def all_barrier(self):
        b = self.barrier()
        for e in ("tensor", "scalar", "vector", "sync", "gpsimd"):
            self.gwait(e, b)



def emit_program(C: Ctx):
    tn = C.tn
    x_sb = tn["x_sb"]
    scr = tn["scratch"]
    sb_bf = scr[:, :].bitcast(BF16)
    sb_f = scr[:, :].bitcast(F32)
    ps = tn["psA"]

    def bank(i, rows=128, cols=512):
        return ps[0:rows, i * 512:i * 512 + cols]

    def bank_row(i, cols=512):
        return ps[0:1, i * 512:i * 512 + cols]

    rows_pool = tn["rows_pool"]

    def row_slot(i, cols=512):
        return rows_pool[0:1, i * 512: i * 512 + cols]

    def ones_row(i):
        return tn["ones1_f"][:, :]

    def v_bf(off_b, cols):
        assert off_b % 2 == 0
        return sb_bf[:, off_b // 2: off_b // 2 + cols]

    def v_f(off_b, cols, p0=0, p1=128):
        assert off_b % 4 == 0
        return sb_f[p0:p1, off_b // 4: off_b // 4 + cols]

    pos_sb = v_f(16384, TPC, 0, 1)
    for name in ["bq_col", "bo_col", "b0a_col", "b1a_col", "b0b_col", "b1b_col",
                 "rms_col", "ones1_f", "i64_sb", "eps_col"]:
        C.dma("sync", tn[name][:, :], tn["dram_" + name].ap())
    C.dma("sync", pos_sb, tn["dram_pos_sb"].ap())
    C.dma("sync", tn["masks_sb"][:, :], tn["dram_masks"].ap())
    C.dma("sync", tn["meta_sb"][:, :], tn["dram_meta"].ap())
    C.dma("sync", tn["ones128_bf"][:, :], tn["dram_ones128_bf"].ap())
    C.dma("sync", tn["eye_sb"][:, :], tn["dram_eye_bf"].ap())
    C.dma("gq", tn["wg_sb"][:, :], tn["wg_cols"].ap())
    for dt in range(DT):
        C.dma("sync", x_sb[:, dt * TPC:(dt + 1) * TPC],
              tn["xT"][dt * 128:(dt + 1) * 128, :])
    sq_consts = C.counts["sq"]
    gq_consts = C.counts["gq"]

    C.vop("tensor_scalar_mul", tn["b0bh_col"][:, :], tn["b0b_col"][:, :], 0.5,
          waits=[("sq", sq_consts)])
    C.vop("tensor_scalar_mul", tn["b1bh_col"][:, :], tn["b1b_col"][:, :], 0.5)

    if _ph_on("tables"):
        C.act(tn["invf_sb"][:, :], tn["i64_sb"][:, :], AF.Exp,
              scale=-math.log(10000.0) / 64.0, waits=[("sq", sq_consts)])
        se_invf = C.counts["se"]
        for cc_ in range(2):
            C.mm(ps[0:64, cc_ * 512:cc_ * 512 + 512], tn["invf_sb"][:, :],
                 pos_sb[0:1, cc_ * 512:(cc_ + 1) * 512],
                 waits=[("se", se_invf)])
        te_ang = C.counts["te"]
        ang = v_f(0, 1024, 0, 64)
        u = v_f(4096, 1024, 0, 64)
        kk = v_f(8192, 1024, 0, 64)
        rr = v_f(12288, 1024, 0, 64)
        C.vop("tensor_copy", ang, ps[0:64, 0:1024], waits=[("te", te_ang)])
        for trig, dst in (("sin", tn["sin_bf"]), ("cos", tn["cos_bf"])):
            off = 0.25 if trig == "cos" else 0.0
            C.vop("tensor_scalar", u, ang, 1.0 / TWO_PI, off, ALU.mult, ALU.add)
            C.vop("tensor_scalar", kk, u, RND_C, RND_C, ALU.add, ALU.subtract)
            C.vop("scalar_tensor_tensor", rr, kk, -CW_HI, ang, ALU.mult, ALU.add)
            C.vop("scalar_tensor_tensor", rr, kk, -CW_MID, rr, ALU.mult, ALU.add)
            ve_red = C.vop("scalar_tensor_tensor", rr, kk, -CW_LO, rr, ALU.mult,
                           ALU.add)
            if trig == "cos":
                ve_red = C.vop("tensor_scalar_add", rr, rr, HALF_PI)
            se_sin = C.act(dst[0:64, :], rr, AF.Sin, waits=[("ve", ve_red)])
            C.vop("tensor_copy", dst[64:128, :], dst[0:64, :],
                  waits=[("se", se_sin)])
            if trig == "sin":
                C.vop("tensor_scalar_mul", dst[0:64, :], dst[0:64, :], -1.0)

    def stats_ln(c0, xb_off, x2_off, want_rms=False):
        bar = C.barrier()
        xb = v_bf(xb_off, 16 * CH)
        x2 = v_bf(x2_off, 16 * CH)
        se_toks = []
        for dt in range(DT):
            src = x_sb[:, dt * TPC + c0: dt * TPC + c0 + CH]
            C.act(xb[:, dt * CH:(dt + 1) * CH], src, AF.Copy,
                  waits=bar if dt == 0 else ())
            se_toks.append(C.act(x2[:, dt * CH:(dt + 1) * CH], src, AF.Square))
        for dt in range(DT):
            w = [("se", se_toks[dt])]
            if dt == 0:
                w += bar
            C.mm(bank_row(0), tn["ones128_bf"][:, :], xb[:, dt * CH:(dt + 1) * CH],
                 start=(dt == 0), stop=(dt == DT - 1), waits=w, inc=(dt == DT - 1))
            C.mm(bank_row(1), tn["ones128_bf"][:, :], x2[:, dt * CH:(dt + 1) * CH],
                 start=(dt == 0), stop=(dt == DT - 1), waits=w, inc=(dt == DT - 1))
        te_sums = C.counts["te"]
        sum_r, sq_r, mean_r = row_slot(0), row_slot(1), row_slot(2)
        var_r, rstd_r, tmp_r = row_slot(3), row_slot(4), row_slot(5)
        C.act(sum_r, bank_row(0), AF.Identity, waits=[("te", te_sums)] + bar)
        se_evac = C.act(sq_r, bank_row(1), AF.Identity)
        C.vop("tensor_scalar_mul", mean_r, sum_r, 1.0 / D,
              waits=[("se", se_evac)] + bar)
        C.vop("tensor_mul", tmp_r, mean_r, mean_r)
        ve_var = C.vop("scalar_tensor_tensor", var_r, sq_r, 1.0 / D, tmp_r,
                       ALU.mult, ALU.subtract)
        C.act(tmp_r, var_r, AF.Ln, bias=tn["eps_col"][0:1, 0:1],
              waits=[("ve", ve_var)])
        se_rstd = C.act(rstd_r, tmp_r, AF.Exp, scale=-0.5)
        se_s2 = None
        if want_rms:
            ms_r = row_slot(6)
            C.vop("tensor_scalar_add", tmp_r, var_r, LN_EPS, waits=[("se", se_rstd)])
            C.vop("reciprocal", sum_r, tmp_r)
            C.vop("tensor_mul", ms_r, var_r, sum_r)
            ve_ms = C.vop("tensor_scalar_add", ms_r, ms_r, RMS_EPS)
            C.act(ms_r, ms_r, AF.Ln, waits=[("ve", ve_ms)])
            se_s2 = C.act(ms_r, ms_r, AF.Exp, scale=-0.5)
        del sum_r, sq_r
        C.mm(bank(2), ones_row(2), mean_r,
             waits=[("se", se_rstd)] + bar)
        te_b = C.mm(bank(3), ones_row(4), rstd_r)
        if want_rms:
            te_b = C.mm(bank(4), ones_row(6), row_slot(6),
                        waits=[("se", se_s2)])
        return te_b

    F_LN, F_TMP, F_H0, F_H1, F_SLAB = 0, 16384, 22528, 55296, 88064

    def load_wslab_a(wname, fs, buf_off, waits):
        tok = None
        for sp in range(4):
            src = tn[wname].ap().rearrange("(a p) f -> a p f", p=128)[
                4 * sp:4 * sp + 4, :, fs * 512:(fs + 1) * 512].rearrange(
                "a p f -> p a f")
            dst = sb_bf[:, buf_off // 2 + sp * 4 * 512:
                        buf_off // 2 + (sp + 1) * 4 * 512].rearrange(
                "p (a f) -> p a f", a=4)
            tok = C.dma("gq", dst, src, waits=waits if sp == 0 else ())
        return tok

    def load_wslab_b(wname, dt, buf_off, waits):
        tok = None
        for sp in range(4):
            src = tn[wname].ap().rearrange("(a p) d -> a p d", p=128)[
                8 * sp:8 * sp + 8, :, dt * 128:(dt + 1) * 128].rearrange(
                "a p d -> p a d")
            dst = sb_bf[:, buf_off // 2 + sp * 8 * 128:
                        buf_off // 2 + (sp + 1) * 8 * 128].rearrange(
                "p (a d) -> p a d", a=8)
            tok = C.dma("gq", dst, src, waits=waits if sp == 0 else ())
        return tok

    slab_guard = {"te": 0}
    wo_pref = {}

    def ffn_pass():
        ffn_pass.calls = getattr(ffn_pass, "calls", -1) + 1
        for ck in range(NCH):
            c0 = ck * CH
            ln_c = v_bf(F_LN, 16 * CH)
            h0c = v_bf(F_H0, 32 * CH)
            h1c = v_bf(F_H1, 32 * CH)
            t1 = v_f(F_TMP, CH)
            e0 = v_f(F_TMP + 2048, CH)
            e1 = v_f(F_TMP + 4096, CH)
            te_b = stats_ln(c0, F_H1, F_H0)
            ve_apply = []
            for dt in range(DT):
                xs = x_sb[:, dt * TPC + c0: dt * TPC + c0 + CH]
                C.vop("tensor_sub", t1, xs, bank(2), waits=[("te", te_b)])
                ve_apply.append(
                    C.vop("tensor_mul", ln_c[:, dt * CH:(dt + 1) * CH], t1, bank(3)))
            ve_ln = ve_apply[-1]
            for dt in range(DT):
                w = [("ve", ve_apply[dt]), ("gq", gq_consts)]
                C.mm(bank_row(0), tn["wg_sb"][:, 2 * dt:2 * dt + 1],
                     ln_c[:, dt * CH:(dt + 1) * CH], start=(dt == 0),
                     stop=(dt == DT - 1), waits=w, inc=(dt == DT - 1))
                C.mm(bank_row(1), tn["wg_sb"][:, 2 * dt + 1:2 * dt + 2],
                     ln_c[:, dt * CH:(dt + 1) * CH], start=(dt == 0),
                     stop=(dt == DT - 1), inc=(dt == DT - 1))
            te_g = C.counts["te"]
            g0_r, m_r = row_slot(0), row_slot(1)
            se_g = C.act(g0_r, bank_row(0), AF.Identity, waits=[("te", te_g)])
            C.vop("tensor_sub", m_r, g0_r, bank_row(1), waits=[("se", se_g)])
            ve_m = C.vop("tensor_single_scalar", m_r, m_r, 0.0, ALU.is_ge)
            te_mb = C.mm(bank(4), ones_row(8), m_r,
                         waits=[("ve", ve_m)])
            ve_m8 = C.vop("tensor_single_scalar", tn["mask_u8"][:, :], bank(4),
                          0.5, ALU.is_ge, waits=[("te", te_mb)])
            if DEBUG_TAP and ck == 0 and ffn_pass.calls == 0:
                C.dma("sync", tn["dbg"].ap()[0:1, 0:3584], rows_pool[0:1, :],
                      waits=[("ve", ve_m8)])
            SL = [F_SLAB, F_SLAB + 16384]
            ready = {}
            consumed = {}
            h_se = {}
            n_slab = 0
            for e in range(2):
                wname = "w0aT" if e == 0 else "w1aT"
                bcol = tn["b0a_col"] if e == 0 else tn["b1a_col"]
                func = AF.Gelu if e == 0 else AF.Relu
                hc = h0c if e == 0 else h1c
                for fs in range(8):
                    buf = n_slab % 2
                    if n_slab >= 2:
                        gw = [("te", consumed[n_slab - 2])]
                    else:
                        gw = [("te", slab_guard["te"])]
                    ready[n_slab] = load_wslab_a(wname, fs, SL[buf], gw)
                    slab = v_bf(SL[buf], 16 * CH)
                    for ft in range(4):
                        fg = fs * 4 + ft
                        bk = (n_slab * 4 + ft) % 2
                        w = [("gq", ready[n_slab]), ("ve", ve_m8)]
                        key = n_slab * 4 + ft
                        if key - 2 in h_se:
                            w.append(("se", h_se[key - 2]))
                        for kt in range(DT):
                            C.mm(bank(bk),
                                 slab[:, kt * 512 + ft * 128: kt * 512 + ft * 128 + 128],
                                 ln_c[:, kt * CH:(kt + 1) * CH],
                                 start=(kt == 0), stop=(kt == DT - 1),
                                 waits=w if kt == 0 else (), inc=(kt == DT - 1))
                        te_h = C.counts["te"]
                        h_se[key] = C.act(hc[:, fg * CH:(fg + 1) * CH], bank(bk),
                                          func, bias=bcol[:, fg:fg + 1],
                                          waits=[("te", te_h)])
                    consumed[n_slab] = C.counts["te"]
                    n_slab += 1
            h_done = (h_se[8 * 4 - 1], h_se[16 * 4 - 1])
            slab_guard["te"] = consumed[15]
            SLB = [F_SLAB, F_SLAB + 8192, F_SLAB + 16384, F_SLAB + 24576]
            bready = {}
            bconsumed = {}
            out_ve = []
            ev_se = []
            n_b = 0
            for dt in range(DT):
                for e in range(2):
                    wname = "w0bT" if e == 0 else "w1bT"
                    buf = n_b % 4
                    if n_b >= 4:
                        gw = [("te", bconsumed[n_b - 4])]
                    else:
                        gw = [("te", slab_guard["te"])]
                    bready[n_b] = load_wslab_b(wname, dt, SLB[buf], gw)
                    slabb = v_bf(SLB[buf], 32 * 128)
                    hc = h0c if e == 0 else h1c
                    bk = (dt % 2) * 2 + e
                    w = [("gq", bready[n_b]), ("se", h_done[e])]
                    if len(out_ve) >= 2:
                        w.append(("ve", out_ve[-2]))
                    for ft in range(FT):
                        C.mm(bank(bk), slabb[:, ft * 128:(ft + 1) * 128],
                             hc[:, ft * CH:(ft + 1) * CH],
                             start=(ft == 0), stop=(ft == FT - 1),
                             waits=w if ft == 0 else (), inc=(ft == FT - 1))
                    bconsumed[n_b] = C.counts["te"]
                    n_b += 1
                te_out = C.counts["te"]
                bk0, bk1 = (dt % 2) * 2, (dt % 2) * 2 + 1
                w = [("te", te_out)]
                if out_ve:
                    w.append(("ve", out_ve[-1]))
                C.act(e0, bank(bk0), AF.Identity, bias=tn["b0bh_col"][:, dt:dt + 1],
                      scale=0.5, waits=w)
                se_e = C.act(e1, bank(bk1), AF.Identity,
                             bias=tn["b1bh_col"][:, dt:dt + 1], scale=0.5)
                xs = x_sb[:, dt * TPC + c0: dt * TPC + c0 + CH]
                C.vop("select", t1, tn["mask_u8"][:, :], e0, e1,
                      waits=[("se", se_e), ("ve", ve_m8)])
                out_ve.append(C.vop("tensor_add", xs, xs, t1))
            slab_guard["te"] = bconsumed[n_b - 1]

    if _ph_on("ffn1"):
        ffn_pass()

    if _ph_on("proj"):
        P_XN, P_PBF, P_TMP, P_PSH, P_RT2, P_KRR = (
            0, 32768, 65536, 69632, 71680, 73728)
        kv_in = tn["kv_in"]
        xn = v_bf(P_XN, 16 * 1024)
        p_bf = v_bf(P_PBF, 16 * 1024)
        t1 = v_f(P_TMP, CH)
        t2 = v_f(P_TMP + 2048, CH)
        psh = v_bf(P_PSH, 1024)
        rt2 = v_bf(P_RT2, 1024)
        kr_ring = [v_bf(P_KRR + i * 2048, 1024) for i in range(2)]
        ve_ln2 = 0
        for ck in range(NCH):
            c0 = ck * CH
            te_b = stats_ln(c0, P_PBF, P_PBF + 16384, want_rms=True)
            for dt in range(DT):
                xs = x_sb[:, dt * TPC + c0: dt * TPC + c0 + CH]
                C.vop("tensor_sub", t1, xs, bank(2), waits=[("te", te_b)])
                C.vop("tensor_mul", t2, t1, bank(3))
                C.vop("tensor_add", xs, xs, t2)
                ve_ln2 = C.vop(
                    "scalar_tensor_tensor",
                    xn[:, dt * 1024 + c0: dt * 1024 + c0 + CH],
                    t2, tn["rms_col"][:, dt:dt + 1], bank(4),
                    ALU.mult, ALU.mult)
        SL = [F_SLAB, F_SLAB + 16384]
        ready = {}
        consumed = {}
        wq_se = {}
        ng = 0
        for fs in range(4):
            buf = fs % 2
            gw = ([("te", consumed[fs - 2])] if fs >= 2
                  else [("te", slab_guard["te"])])
            ready[fs] = load_wslab_a("wqT", fs, SL[buf], gw)
            slab = v_bf(SL[buf], 16 * CH)
            for ft in range(4):
                fg = fs * 4 + ft
                for ck in range(NCH):
                    c0 = ck * CH
                    bk = 4 + (ng % 2)
                    w = [("gq", ready[fs]), ("ve", ve_ln2)]
                    if ng - 2 in wq_se:
                        w.append(("se", wq_se[ng - 2]))
                    for kt in range(DT):
                        C.mm(bank(bk),
                             slab[:, kt * 512 + ft * 128: kt * 512 + ft * 128 + 128],
                             xn[:, kt * 1024 + c0: kt * 1024 + c0 + CH],
                             start=(kt == 0), stop=(kt == DT - 1),
                             waits=w if kt == 0 else (), inc=(kt == DT - 1))
                    te_w = C.counts["te"]
                    wq_se[ng] = C.act(
                        p_bf[:, fg * 1024 + c0: fg * 1024 + c0 + CH], bank(bk),
                        AF.Identity, bias=tn["bq_col"][:, fg:fg + 1],
                        waits=[("te", te_w)])
                    ng += 1
            consumed[fs] = C.counts["te"]
        slab_guard["te"] = consumed[3]
        se_wq_of = {}
        for h in range(H):
            se_wq_of[h] = wq_se[(h // 4) * 8 + (h % 4) * 2 + 1]
        se_wq = wq_se[ng - 1]
        cosb, sinb = tn["cos_bf"], tn["sin_bf"]
        kv_stage = {}
        for h in range(H):
            krh = kr_ring[h % 2]
            w = [("se", se_wq_of[h])]
            if h >= 2:
                w.append(("sq", kv_stage[h - 2]))
            C.vop("tensor_copy", psh[0:64, :],
                  p_bf[64:128, h * 1024:(h + 1) * 1024], waits=w)
            C.vop("tensor_copy", psh[64:128, :],
                  p_bf[0:64, h * 1024:(h + 1) * 1024])
            C.vop("tensor_mul", krh, p_bf[:, h * 1024:(h + 1) * 1024],
                  cosb[:, :])
            C.vop("tensor_mul", rt2, psh, sinb[:, :])
            ve_kr = C.vop("tensor_add", krh, krh, rt2)
            kv_stage[h] = C.dma(
                "sync", kv_in.ap()[128 * h: 128 * (h + 1), :], krh,
                waits=[("ve", ve_kr)])
        kr_staged = kv_stage[H - 1]
        C.coll("AllGather", [kv_in.ap()[0:2048, :].opt()],
               [tn["kr_gout"].ap().opt()], waits=[("sq", kr_staged)])
        P_VST = 77824
        vstage = [v_bf(P_VST + i * 4096, 2048) for i in range(2)]
        kv_in_flat = kv_in.ap().rearrange("r c -> (r c)")
        vt_grp_se = {}
        vt_grp_te = {}
        dma_vst = {}
        v_staged = 0
        gidx = 0
        for tt in range(8):
            for sub in range(2):
                bko = (gidx % 2) * 1024
                for i in range(8):
                    h = sub * 8 + i
                    w = [("se", se_wq_of[h])]
                    if gidx >= 2 and i == 0:
                        w.append(("se", vt_grp_se[gidx - 2]))
                    C.transpose(
                        tn["psT"][:, bko + i * 128: bko + (i + 1) * 128],
                        p_bf[:, h * 1024 + tt * 128: h * 1024 + tt * 128 + 128],
                        waits=w, inc=(i == 7))
                vt_grp_te[gidx] = C.counts["te"]
                for i in range(8):
                    h = sub * 8 + i
                    w = [("te", vt_grp_te[gidx])] if i == 0 else []
                    if tt >= 2 and i == 0:
                        w.append(("sq", dma_vst[tt - 2]))
                    vt_grp_se[gidx] = C.act(
                        vstage[tt % 2][:, h * 128:(h + 1) * 128],
                        tn["psT"][:, bko + i * 128: bko + (i + 1) * 128],
                        AF.Copy, waits=w)
                gidx += 1
            dst = kv_in_flat.rearrange("(half t d) -> half t d",
                                       half=2, t=1024, d=2048)[
                1, tt * 128:(tt + 1) * 128, :]
            dma_vst[tt] = C.dma("sync", dst, vstage[tt % 2][:, :],
                                waits=[("se", vt_grp_se[gidx - 1])])
            v_staged = dma_vst[tt]
        C.coll("AllGather", [kv_in.ap()[2048:4096, :].opt()],
               [tn["v_gout"].ap().opt()], waits=[("sq", v_staged)])

    def attn_phase():
        A_KR, A_V, A_A, A_OST, A_SQ, A_RB = (
            0, 32768, 65536, 73728, 77824, 81920)
        kr_sb = v_bf(A_KR, 2 * S)
        v_sb = v_bf(A_V, 2 * S)
        a_ring = [v_bf(A_A + i * 2048, 1024) for i in range(2)]
        o_stage = [v_bf(A_OST + i * 1024, CH) for i in range(2)]
        sq_scr = v_bf(A_SQ, 2048)
        rb_sb = [v_f(A_RB + i * 2048, CH) for i in range(2)]
        o_in = tn["o_in"]
        bar = C.barrier()
        C.gwait("sync", bar)
        C.gwait("gpsimd", bar)
        if ATTN_UPTO == "gather":
            return
        gq_kr = 0
        for jh in range(2):
            dst = kr_sb[:, jh * S: (jh + 1) * S].rearrange("p (r t) -> p r t", r=8)
            gq_kr = C.gdma_off(
                dst, tn["kr_gout"], jh,
                [[1024, 128], [2048 * 1024, 8], [1, 1024]],
                waits=[("cc", C.counts["cc"] - 1), ("sq", sq_consts)])
        for h in range(HPC):
            for cc_ in range(4):
                C.act(sq_scr, kr_sb[:, h * S + cc_ * 2048: h * S + (cc_ + 1) * 2048],
                      AF.Square,
                      waits=([("gq", gq_kr)] + bar) if (cc_ == 0 and h == 0) else ())
                se_sq = C.counts["se"]
                for k4 in range(4):
                    C.mm(bank_row(0), tn["ones128_bf"][:, :],
                         sq_scr[:, k4 * 512:(k4 + 1) * 512],
                         waits=[("se", se_sq), ("ve", C.counts["ve"])] + (
                             bar if (cc_ == 0 and h == 0 and k4 == 0) else []))
                    te_n = C.counts["te"]
                    C.vop("reduce_max",
                          row_slot(1)[0:1, 16 * h + 4 * cc_ + k4:
                                      16 * h + 4 * cc_ + k4 + 1],
                          bank_row(0), axis=AX.X, waits=[("te", te_n)])
        C.vop("reduce_max", row_slot(2)[0:1, 0:1], row_slot(1)[0:1, 0:32],
              axis=AX.X)
        ve_c = C.vop("tensor_scalar_mul", row_slot(2)[0:1, 0:1],
                     row_slot(2)[0:1, 0:1], -INV_SQRT_DK)
        te_c = C.mm(bank(0, 128, 1), ones_row(2),
                    row_slot(2)[0:1, 0:1], waits=[("ve", ve_c)])
        se_negc = C.act(tn["negC_col"][:, 0:1], bank(0, 128, 1), AF.Copy,
                        waits=[("te", te_c)])
        gq_kv = 0
        for jh in range(2):
            dst = v_sb[:, jh * S: (jh + 1) * S]
            gq_kv = C.gdma_off(
                dst, tn["v_gout"], 2 + jh,
                [[2048, 128], [128 * 2048, 64], [1, 128]],
                waits=[("cc", C.counts["cc"])] if jh == 0 else ())
        se_vt = 0
        if ATTN_UPTO in ("loads", "vt", "cbound"):
            C.gwait("sync", [("gq", gq_kv), ("se", se_negc)])
            return
        ve_precalc = C.counts["ve"]
        if ATTN_UPTO == "cbound":
            C.gwait("sync", [("se", C.counts["se"]), ("ve", ve_precalc)])
            return
        s_guard_se = 0
        a_guard_te = [0, 0]
        o_guard_ve = [0, 0]
        rs_guard_se = [0, 0]
        rs_row_guard_ve = [0, 0]
        rec_row_guard_te = [0, 0]
        ost_guard_sq = [0, 0]
        rb_guard_ve = [0, 0]
        pend_epi = [None]
        first_s = True
        for qc in range(16):
            E = 4 * qc + 4
            exp_toks = {}
            s_toks = {}
            av_w = {}

            def emit_avrs(kt):
                for h in range(HPC):
                    pair = kt % 2
                    w = list(av_w[kt]) if h == 0 else []
                    if kt == 0:
                        w += [("ve", o_guard_ve[h]), ("se", rs_guard_se[h])]
                    C.mm(bank(2 + h),
                         v_sb[:, (h * 64 + kt) * 128:(h * 64 + kt + 1) * 128],
                         a_ring[pair][:, h * CH:(h + 1) * CH],
                         start=(kt == 0), stop=(kt == E - 1), waits=w)
                    a_guard_te[pair] = C.mm(
                        bank_row(4 + h), tn["ones128_bf"][:, :],
                        a_ring[pair][:, h * CH:(h + 1) * CH],
                        start=(kt == 0), stop=(kt == E - 1))

            for kt in range(E):
                for h in range(HPC):
                    w = [("se", se_negc), ("se", s_guard_se)]
                    if first_s:
                        w += [("gq", gq_kv), ("ve", C.counts["ve"])]
                        first_s = False
                    s_toks[(kt, h)] = C.mm(
                        bank(h),
                        kr_sb[:, h * S + kt * 128: h * S + kt * 128 + 128],
                        kr_sb[:, h * S + qc * 512: h * S + qc * 512 + 512],
                        waits=w)
                if kt == 1 and pend_epi[0] is not None:
                    pend_epi[0]()
                    pend_epi[0] = None
                pair = kt % 2
                w = [("te", s_toks[(kt, 1)]), ("te", a_guard_te[pair])]
                exp_toks[kt] = C.act(
                    a_ring[pair], ps[0:128, 0:1024], AF.Exp,
                    bias=tn["negC_col"][:, 0:1], scale=INV_SQRT_DK, waits=w)
                s_guard_se = exp_toks[kt]
                if kt >= E - 4:
                    j = kt - 4 * qc
                    for h in range(HPC):
                        ve_mask = C.vop(
                            "tensor_mul", a_ring[pair][:, h * CH:(h + 1) * CH],
                            a_ring[pair][:, h * CH:(h + 1) * CH],
                            tn["masks_sb"][:, j * 512:(j + 1) * 512],
                            waits=[("se", exp_toks[kt])] if h == 0 else ())
                    av_w[kt] = [("ve", ve_mask)]
                else:
                    av_w[kt] = [("se", exp_toks[kt])]
                if kt >= 1:
                    emit_avrs(kt - 1)
            emit_avrs(E - 1)
            te_avdone = C.counts["te"]
            ve_recs = [0, 0]
            for h in range(HPC):
                rs_r = row_slot(3 + h)
                rec_r = row_slot(5 + h)
                se_rs = C.act(rs_r, bank_row(4 + h), AF.Identity,
                              waits=[("te", te_avdone),
                                     ("ve", rs_row_guard_ve[h])])
                rs_row_guard_ve[h] = ve_recs[h] = C.vop(
                    "reciprocal", rec_r, rs_r,
                    waits=[("se", se_rs), ("te", rec_row_guard_te[h])])

            def make_epi(qc=qc, ve_recs=tuple(ve_recs)):
                def epi():
                    for h in range(HPC):
                        rec_r = row_slot(5 + h)
                        te_bc = C.mm(bank(4 + h), ones_row(5 + h), rec_r,
                                     waits=[("ve", ve_recs[h])])
                        rec_row_guard_te[h] = te_bc
                        se_cp = C.act(rb_sb[h], bank(4 + h), AF.Copy,
                                      waits=[("te", te_bc),
                                             ("ve", rb_guard_ve[h])])
                        rs_guard_se[h] = se_cp
                        ve_o = C.vop("tensor_mul", o_stage[h], bank(2 + h),
                                     rb_sb[h],
                                     waits=[("se", se_cp),
                                            ("sq", ost_guard_sq[h])])
                        o_guard_ve[h] = ve_o
                        rb_guard_ve[h] = ve_o
                        ost_guard_sq[h] = C.dma(
                            "sync",
                            o_in.ap()[128 * h: 128 * (h + 1),
                                      qc * 512: qc * 512 + 512],
                            o_stage[h], waits=[("ve", ve_o)])
                return epi

            pend_epi[0] = make_epi()
        if pend_epi[0] is not None:
            pend_epi[0]()
            pend_epi[0] = None
        o_dma_last = max(ost_guard_sq)
        if ATTN_UPTO == "main":
            C.gwait("sync", [("sq", o_dma_last)])
            return
        if _ph_on("wo"):
            wo_pref[0] = load_wslab_a("woT", 0, F_SLAB,
                                      [("te", slab_guard["te"])])
            wo_pref[1] = load_wslab_a("woT", 1, F_SLAB + 16384, [])
        C.coll("AllGather", [o_in.ap().opt()], [tn["o_gout"].ap().opt()],
               waits=[("sq", o_dma_last)])

    if _ph_on("attn"):
        attn_phase()

    if _ph_on("wo"):
        W_OFM, W_TMP = 0, 32768
        o_fm = v_bf(W_OFM, 16 * 1024)
        e0 = v_f(W_TMP, CH)
        bar = C.barrier()
        C.gwait("sync", bar)
        C.gwait("gpsimd", bar)
        sq_ofm = 0
        for gh in range(2):
            dst = o_fm[:, gh * 8192:(gh + 1) * 8192].rearrange(
                "p (g t) -> p g t", g=8)
            sq_ofm = C.gdma_off(dst, tn["o_gout"], 10 + gh,
                                [[8192, 128], [128 * 8192, 8], [1, 1024]],
                                waits=[("cc", C.counts["cc"])])
        SL = [F_SLAB, F_SLAB + 16384]
        consumed = {}
        wo_ve = []
        ng = 0
        for fs in range(4):
            buf = fs % 2
            if fs in wo_pref:
                ready = wo_pref[fs]
            else:
                gw = ([("te", consumed[fs - 2])] if fs >= 2
                      else [("te", slab_guard["te"])])
                ready = load_wslab_a("woT", fs, SL[buf], gw)
            slab = v_bf(SL[buf], 16 * CH)
            for ft in range(4):
                fg = fs * 4 + ft
                for ck in range(NCH):
                    c0 = ck * CH
                    bk = 4 + (ng % 2)
                    w = [("gq", ready), ("gq", sq_ofm)]
                    if ng == 0:
                        w += bar
                    if len(wo_ve) >= 2:
                        w.append(("ve", wo_ve[-2]))
                    for kt in range(DT):
                        C.mm(bank(bk),
                             slab[:, kt * 512 + ft * 128: kt * 512 + ft * 128 + 128],
                             o_fm[:, kt * 1024 + c0: kt * 1024 + c0 + CH],
                             start=(kt == 0), stop=(kt == DT - 1),
                             waits=w if kt == 0 else (), inc=(kt == DT - 1))
                    te_w = C.counts["te"]
                    wse = [("te", te_w)]
                    if wo_ve:
                        wse.append(("ve", wo_ve[-1]))
                    se_e = C.act(e0, bank(bk), AF.Identity,
                                 bias=tn["bo_col"][:, fg:fg + 1], waits=wse)
                    xs = x_sb[:, fg * TPC + c0: fg * TPC + c0 + CH]
                    wo_ve.append(C.vop("tensor_add", xs, xs, e0,
                                       waits=[("se", se_e)]))
                    ng += 1
            consumed[fs] = C.counts["te"]
        slab_guard["te"] = consumed[3]

    if _ph_on("ffn2"):
        ffn_pass()

    bar = C.barrier()
    st = 0
    for dt in range(DT):
        st = C.dma("sync", tn["out"].ap()[dt * 128:(dt + 1) * 128, :],
                   x_sb[:, dt * TPC:(dt + 1) * TPC], waits=bar if dt == 0 else ())
    C.gwait("sync", [("sq", st)])
    C.gwait("gpsimd", [("sq", st)])



def build():
    nc = bass.Bass("TRN2", target_bir_lowering=False)
    tn = {}

    tn["xT"] = nc.dram_tensor("xT", [D, TPC], F32, kind="ExternalInput").ap()
    tn["out"] = nc.dram_tensor("out_y", [D, TPC], F32, kind="ExternalOutput")
    tn["dbg"] = nc.dram_tensor("dbg", [128, 4096], F32, kind="ExternalOutput")
    tn["wqT"] = nc.dram_tensor("wqT", [D, D], F32, kind="ExternalInput")
    tn["woT"] = nc.dram_tensor("woT", [D, D], F32, kind="ExternalInput")
    tn["w0aT"] = nc.dram_tensor("w0aT", [D, FF], F32, kind="ExternalInput")
    tn["w0bT"] = nc.dram_tensor("w0bT", [FF, D], F32, kind="ExternalInput")
    tn["w1aT"] = nc.dram_tensor("w1aT", [D, FF], F32, kind="ExternalInput")
    tn["w1bT"] = nc.dram_tensor("w1bT", [FF, D], F32, kind="ExternalInput")
    tn["wg_cols"] = nc.dram_tensor("wg_cols", [128, 32], F32, kind="ExternalInput")
    for name, k in [("bq_col", 16), ("bo_col", 16), ("b0a_col", 32),
                    ("b1a_col", 32), ("b0b_col", 16), ("b1b_col", 16),
                    ("rms_col", 16)]:
        tn["dram_" + name] = nc.dram_tensor(name, [128, k], F32,
                                            kind="ExternalInput")
    tn["dram_pos_sb"] = nc.dram_tensor("pos", [1, TPC], F32, kind="ExternalInput")

    tn["dram_ones1_f"] = nc.inline_tensor(np.ones((1, 128), np.float32), "c_ones1")
    tn["dram_i64_sb"] = nc.inline_tensor(
        np.arange(64, dtype=np.float32).reshape(1, 64), "c_i64")
    tn["dram_eps_col"] = nc.inline_tensor(
        np.full((1, 1), LN_EPS, np.float32), "c_eps")
    tn["dram_ones128_bf"] = nc.inline_tensor(
        np.ones((128, 1), ml_dtypes.bfloat16), "c_ones128")
    tn["dram_eye_bf"] = nc.inline_tensor(
        np.eye(128, dtype=ml_dtypes.bfloat16), "c_eye")
    masks = np.zeros((128, 4 * 512), np.float32)
    for j in range(4):
        k_idx = np.arange(128)[:, None]
        q_idx = np.arange(512)[None, :]
        masks[:, j * 512:(j + 1) * 512] = (j * 128 + k_idx <= q_idx)
    tn["dram_masks"] = nc.inline_tensor(masks.astype(ml_dtypes.bfloat16), "c_masks")

    tn["kv_in"] = nc.dram_tensor("kv_in", [4096, 1024], BF16)
    tn["kr_gout"] = nc.dram_tensor("kr_gout", [8 * 2048, 1024], BF16,
                                   addr_space="Shared")
    tn["v_gout"] = nc.dram_tensor("v_gout", [8 * 2048, 1024], BF16,
                                  addr_space="Shared")
    tn["o_in"] = nc.dram_tensor("o_in", [256, 8192], BF16)
    tn["o_gout"] = nc.dram_tensor("o_gout", [2048, 8192], BF16,
                                  addr_space="Shared")
    tn["dram_meta"] = nc.dram_tensor("meta", [1, 64], mybir.dt.uint32,
                                     kind="ExternalInput")

    with ExitStack() as ctx:
        e = ctx.enter_context
        tn["x_sb"] = e(nc.sbuf_tensor("x_sb", [128, DT * TPC], F32))
        tn["masks_sb"] = e(nc.sbuf_tensor("masks_sb", [128, 2048], BF16))
        tn["ones128_bf"] = e(nc.sbuf_tensor("ones128_bf", [128, 1], BF16))
        tn["eye_sb"] = e(nc.sbuf_tensor("eye_sb", [128, 128], BF16))
        tn["ones1_f"] = e(nc.sbuf_tensor("ones1_f", [1, 128], F32))
        for name, k in [("bq_col", 16), ("bo_col", 16), ("b0a_col", 32),
                        ("b1a_col", 32), ("b0b_col", 16), ("b1b_col", 16),
                        ("rms_col", 16), ("b0bh_col", 16), ("b1bh_col", 16)]:
            tn[name] = e(nc.sbuf_tensor(name + "_sb", [128, k], F32))
        tn["wg_sb"] = e(nc.sbuf_tensor("wg_sb", [128, 32], BF16))
        tn["i64_sb"] = e(nc.sbuf_tensor("i64_sb", [1, 64], F32))
        tn["invf_sb"] = e(nc.sbuf_tensor("invf_sb", [1, 64], F32))
        tn["rows_pool"] = e(nc.sbuf_tensor("rows_pool", [1, 3584], F32))
        tn["cos_bf"] = e(nc.sbuf_tensor("cos_bf", [128, TPC], BF16))
        tn["sin_bf"] = e(nc.sbuf_tensor("sin_bf", [128, TPC], BF16))
        tn["negC_col"] = e(nc.sbuf_tensor("negC_col", [128, 2], F32))
        tn["mask_u8"] = e(nc.sbuf_tensor("mask_u8", [128, 512], U8))
        tn["eps_col"] = e(nc.sbuf_tensor("eps_col", [1, 1], F32))
        tn["meta_sb"] = e(nc.sbuf_tensor("meta_sb", [1, 64], mybir.dt.uint32))
        SCR = 120832
        tn["scratch"] = e(nc.sbuf_tensor("scratch", [128, SCR], U8))
        tn["psA"] = e(nc.psum_tensor("psA", [128, 3072], F32))
        tn["psT"] = e(nc.psum_tensor("psT", [128, 2048], BF16))

        sems = {}
        for s_ in ENGS:
            sems[s_] = e(nc.semaphore("sem_" + s_))
        block = e(nc.Block())

        def run(eng_name):
            def f(eng):
                C = Ctx(nc, eng_name, eng, sems, tn)
                if eng_name == "gpsimd":
                    with eng.register("roff") as roff:
                        C.roff = roff
                        emit_program(C)
                else:
                    emit_program(C)
            return f

        block.gpsimd(run("gpsimd"))
        block.scalar(run("scalar"))
        block.tensor(run("tensor"))
        block.vector(run("vector"))
        block.sync(run("sync"))

    return nc


_CACHED = {}


def kernel(**inputs):
    X = np.asarray(inputs["L0_attn_X"])
    asc = np.ascontiguousarray

    def colpack(v, k):
        return asc(np.asarray(v, np.float32).reshape(k, 128).T)

    shared = {
        "wqT": asc(np.asarray(inputs["L0_attn_Wq_w"], np.float32).T),
        "woT": asc(np.asarray(inputs["L0_attn_Wo_w"], np.float32).T),
        "w0aT": asc(np.asarray(inputs["L0_ffn_W0a"], np.float32).T),
        "w0bT": asc(np.asarray(inputs["L0_ffn_W0b"], np.float32).T),
        "w1aT": asc(np.asarray(inputs["L0_ffn_W1a"], np.float32).T),
        "w1bT": asc(np.asarray(inputs["L0_ffn_W1b"], np.float32).T),
        "wg_cols": asc(np.asarray(inputs["L0_ffn_Wg"], np.float32).T
                       .reshape(16, 128, 2).transpose(1, 0, 2).reshape(128, 32)),
        "bq_col": colpack(inputs["L0_attn_Wq_b"], 16),
        "bo_col": colpack(inputs["L0_attn_Wo_b"], 16),
        "b0a_col": colpack(inputs["L0_ffn_b0a"], 32),
        "b1a_col": colpack(inputs["L0_ffn_b1a"], 32),
        "b0b_col": colpack(inputs["L0_ffn_b0b"], 16),
        "b1b_col": colpack(inputs["L0_ffn_b1b"], 16),
        "rms_col": colpack(inputs["L0_attn_rms_w"], 16),
    }
    in_maps = []
    for c in range(NCORE):
        m = dict(shared)
        m["xT"] = asc(X[0, c * TPC:(c + 1) * TPC, :].T)
        m["pos"] = np.arange(c * TPC, (c + 1) * TPC,
                             dtype=np.float32).reshape(1, TPC)
        meta = np.zeros(64, np.uint32)
        for jh in range(2):
            meta[jh] = (256 * c + 128 * jh) * 1024
            meta[2 + jh] = 256 * c + 128 * jh
        for gh in range(2):
            meta[10 + gh] = 1024 * c + gh * 8 * 128 * 8192
        m["meta"] = meta.reshape(1, 64)
        in_maps.append(m)

    if "nc" not in _CACHED:
        _CACHED["nc"] = build()
    res = run_bass_kernel_spmd(_CACHED["nc"], in_maps, core_ids=list(range(NCORE)))
    y = np.empty((S, D), np.float32)
    for c in range(NCORE):
        y[c * TPC:(c + 1) * TPC, :] = res.results[c]["out_y"].T
    return y.reshape(1, S, D)


# version 05
# speedup vs baseline: 1.0298x; 1.0298x over previous
import math
from contextlib import ExitStack

import numpy as np
import ml_dtypes

import concourse.bass as bass
import concourse.mybir as mybir
from concourse.bass_utils import run_bass_kernel_spmd

F32 = mybir.dt.float32
BF16 = mybir.dt.bfloat16
U8 = mybir.dt.uint8
AF = mybir.ActivationFunctionType
ALU = mybir.AluOpType
AX = mybir.AxisListType

S = 8192
D = 2048
TPC = 1024
NCORE = 8
H = 16
DK = 128
HPC = 2
DT = 16
FF = 4096
FT = 32
CH = 512
NCH = 2
INV_SQRT_DK = 1.0 / math.sqrt(DK)
LN_EPS = 1e-5
RMS_EPS = float(np.finfo(np.float32).eps)

TWO_PI = 2.0 * math.pi
CW_HI = 6.28125
CW_MID = float(np.float32(TWO_PI - CW_HI))
CW_LO = float(np.float32(TWO_PI - CW_HI - float(np.float32(TWO_PI - CW_HI))))
RND_C = float(np.float32(2.0 ** 23))
HALF_PI = float(np.float32(math.pi / 2.0))

DEBUG_TAP = False
ATTN_UPTO = "all"
BUILD_UPTO = "ffn2"
_PHASE_ORDER = ["tables", "ffn1", "proj", "attn", "wo", "ffn2"]

ENGS = ("te", "se", "ve", "sq", "gq", "cc")


def _ph_on(name):
    return (_PHASE_ORDER.index(name) <= _PHASE_ORDER.index(BUILD_UPTO))


class Ctx:

    def __init__(self, nc, eng_name, eng, sems, tn):
        self.nc = nc
        self.eng_name = eng_name
        self.eng = eng
        self.sems = sems
        self.tn = tn
        self.counts = {k: 0 for k in sems}
        self.last_wait = {k: 0 for k in sems}

    def on(self, name):
        return self.eng_name == name

    def _waits(self, waits):
        for s, v in waits:
            if v is None or v <= 0:
                continue
            if v > self.counts[s]:
                raise RuntimeError(f"wait on future {s}: {v} > {self.counts[s]}")
            if v > self.last_wait[s]:
                self.eng.wait_ge(self.sems[s], v)
                self.last_wait[s] = v

    def bump(self, sem, n):
        self.counts[sem] += n
        return self.counts[sem]

    def barrier(self):
        return [(s, self.counts[s]) for s in ("te", "se", "ve", "sq", "gq", "cc")]

    def dma(self, queue, out, in_, waits=(), **kw):
        sem = "sq" if queue == "sync" else "gq"
        if self.on("sync" if queue == "sync" else "gpsimd"):
            self._waits(waits)
            self.eng.dma_start(out=out, in_=in_, **kw).then_inc(self.sems[sem], 16)
        return self.bump(sem, 16)

    def mm(self, out, lhsT, rhs, start=True, stop=True, waits=(), inc=True):
        if self.on("tensor"):
            self._waits(waits)
            i = self.eng.matmul(out, lhsT, rhs, start=start, stop=stop)
            if inc:
                i.then_inc(self.sems["te"], 1)
        return self.bump("te", 1) if inc else None

    def transpose(self, out, in_, waits=(), inc=True):
        if self.on("tensor"):
            self._waits(waits)
            i = self.eng.transpose(out, in_, self.tn["eye_sb"][:, :])
            if inc:
                i.then_inc(self.sems["te"], 1)
        return self.bump("te", 1) if inc else None

    def act(self, out, in_, func, bias=0.0, scale=1.0, accum_out=None, waits=(),
            inc=True):
        if self.on("scalar"):
            self._waits(waits)
            i = self.eng.activation(out, in_, func, bias=bias, scale=scale,
                                    accum_out=accum_out)
            if inc:
                i.then_inc(self.sems["se"], 1)
        return self.bump("se", 1) if inc else None

    def vop(self, fn, *args, waits=(), inc=True, **kw):
        if self.on("vector"):
            self._waits(waits)
            i = getattr(self.eng, fn)(*args, **kw)
            if inc:
                i.then_inc(self.sems["ve"], 1)
        return self.bump("ve", 1) if inc else None

    def coll(self, kind, ins, outs, waits=()):
        if self.on("gpsimd"):
            self._waits(waits)
            self.eng.collective_compute(
                kind, ALU.bypass, replica_groups=[list(range(NCORE))],
                ins=ins, outs=outs,
            ).then_inc(self.sems["cc"])
        return self.bump("cc", 1)

    def gdma_off(self, out, dram_tensor, meta_idx, pattern, waits=()):
        if self.on("gpsimd"):
            self._waits(waits)
            self.eng.reg_load(self.roff, self.tn["meta_sb"][0:1,
                                                            meta_idx:meta_idx + 1])
            src = bass.AP(dram_tensor, self.roff, pattern)
            self.eng.dma_start(out=out, in_=src).then_inc(self.sems["gq"], 16)
        return self.bump("gq", 16)

    def gwait(self, eng_name, waits):
        if self.on(eng_name):
            self._waits(waits)

    def all_barrier(self):
        b = self.barrier()
        for e in ("tensor", "scalar", "vector", "sync", "gpsimd"):
            self.gwait(e, b)



def emit_program(C: Ctx):
    tn = C.tn
    x_sb = tn["x_sb"]
    scr = tn["scratch"]
    sb_bf = scr[:, :].bitcast(BF16)
    sb_f = scr[:, :].bitcast(F32)
    ps = tn["psA"]

    def bank(i, rows=128, cols=512):
        return ps[0:rows, i * 512:i * 512 + cols]

    def bank_row(i, cols=512):
        return ps[0:1, i * 512:i * 512 + cols]

    rows_pool = tn["rows_pool"]

    def row_slot(i, cols=512):
        return rows_pool[0:1, i * 512: i * 512 + cols]

    def ones_row(i):
        return tn["ones1_f"][:, :]

    def v_bf(off_b, cols):
        assert off_b % 2 == 0
        return sb_bf[:, off_b // 2: off_b // 2 + cols]

    def v_f(off_b, cols, p0=0, p1=128):
        assert off_b % 4 == 0
        return sb_f[p0:p1, off_b // 4: off_b // 4 + cols]

    pos_sb = v_f(16384, TPC, 0, 1)
    for name in ["bq_col", "bo_col", "b0a_col", "b1a_col", "b0b_col", "b1b_col",
                 "rms_col", "ones1_f", "i64_sb", "eps_col"]:
        C.dma("sync", tn[name][:, :], tn["dram_" + name].ap())
    C.dma("sync", pos_sb, tn["dram_pos_sb"].ap())
    C.dma("sync", tn["masks_sb"][:, :], tn["dram_masks"].ap())
    C.dma("sync", tn["meta_sb"][:, :], tn["dram_meta"].ap())
    C.dma("sync", tn["ones128_bf"][:, :], tn["dram_ones128_bf"].ap())
    C.dma("sync", tn["eye_sb"][:, :], tn["dram_eye_bf"].ap())
    C.dma("gq", tn["wg_sb"][:, :], tn["wg_cols"].ap())
    for dt in range(DT):
        C.dma("sync", x_sb[:, dt * TPC:(dt + 1) * TPC],
              tn["xT"][dt * 128:(dt + 1) * 128, :])
    sq_consts = C.counts["sq"]
    gq_consts = C.counts["gq"]

    C.vop("tensor_scalar_mul", tn["b0bh_col"][:, :], tn["b0b_col"][:, :], 0.5,
          waits=[("sq", sq_consts)])
    C.vop("tensor_scalar_mul", tn["b1bh_col"][:, :], tn["b1b_col"][:, :], 0.5)

    if _ph_on("tables"):
        C.act(tn["invf_sb"][:, :], tn["i64_sb"][:, :], AF.Exp,
              scale=-math.log(10000.0) / 64.0, waits=[("sq", sq_consts)])
        se_invf = C.counts["se"]
        for cc_ in range(2):
            C.mm(ps[0:64, cc_ * 512:cc_ * 512 + 512], tn["invf_sb"][:, :],
                 pos_sb[0:1, cc_ * 512:(cc_ + 1) * 512],
                 waits=[("se", se_invf)])
        te_ang = C.counts["te"]
        ang = v_f(0, 1024, 0, 64)
        u = v_f(4096, 1024, 0, 64)
        kk = v_f(8192, 1024, 0, 64)
        rr = v_f(12288, 1024, 0, 64)
        C.vop("tensor_copy", ang, ps[0:64, 0:1024], waits=[("te", te_ang)])
        for trig, dst in (("sin", tn["sin_bf"]), ("cos", tn["cos_bf"])):
            off = 0.25 if trig == "cos" else 0.0
            C.vop("tensor_scalar", u, ang, 1.0 / TWO_PI, off, ALU.mult, ALU.add)
            C.vop("tensor_scalar", kk, u, RND_C, RND_C, ALU.add, ALU.subtract)
            C.vop("scalar_tensor_tensor", rr, kk, -CW_HI, ang, ALU.mult, ALU.add)
            C.vop("scalar_tensor_tensor", rr, kk, -CW_MID, rr, ALU.mult, ALU.add)
            ve_red = C.vop("scalar_tensor_tensor", rr, kk, -CW_LO, rr, ALU.mult,
                           ALU.add)
            if trig == "cos":
                ve_red = C.vop("tensor_scalar_add", rr, rr, HALF_PI)
            se_sin = C.act(dst[0:64, :], rr, AF.Sin, waits=[("ve", ve_red)])
            C.vop("tensor_copy", dst[64:128, :], dst[0:64, :],
                  waits=[("se", se_sin)])
            if trig == "sin":
                C.vop("tensor_scalar_mul", dst[0:64, :], dst[0:64, :], -1.0)

    def stats_ln(c0, xb_off, x2_off, want_rms=False):
        bar = C.barrier()
        xb = v_bf(xb_off, 16 * CH)
        x2 = v_bf(x2_off, 16 * CH)
        se_toks = []
        for dt in range(DT):
            src = x_sb[:, dt * TPC + c0: dt * TPC + c0 + CH]
            C.act(xb[:, dt * CH:(dt + 1) * CH], src, AF.Copy,
                  waits=bar if dt == 0 else ())
            se_toks.append(C.act(x2[:, dt * CH:(dt + 1) * CH], src, AF.Square))
        for dt in range(DT):
            w = [("se", se_toks[dt])]
            if dt == 0:
                w += bar
            C.mm(bank_row(0), tn["ones128_bf"][:, :], xb[:, dt * CH:(dt + 1) * CH],
                 start=(dt == 0), stop=(dt == DT - 1), waits=w, inc=(dt == DT - 1))
            C.mm(bank_row(1), tn["ones128_bf"][:, :], x2[:, dt * CH:(dt + 1) * CH],
                 start=(dt == 0), stop=(dt == DT - 1), waits=w, inc=(dt == DT - 1))
        te_sums = C.counts["te"]
        sum_r, sq_r, mean_r = row_slot(0), row_slot(1), row_slot(2)
        var_r, rstd_r, tmp_r = row_slot(3), row_slot(4), row_slot(5)
        C.act(sum_r, bank_row(0), AF.Identity, waits=[("te", te_sums)] + bar)
        se_evac = C.act(sq_r, bank_row(1), AF.Identity)
        C.vop("tensor_scalar_mul", mean_r, sum_r, 1.0 / D,
              waits=[("se", se_evac)] + bar)
        C.vop("tensor_mul", tmp_r, mean_r, mean_r)
        ve_var = C.vop("scalar_tensor_tensor", var_r, sq_r, 1.0 / D, tmp_r,
                       ALU.mult, ALU.subtract)
        C.act(tmp_r, var_r, AF.Ln, bias=tn["eps_col"][0:1, 0:1],
              waits=[("ve", ve_var)])
        se_rstd = C.act(rstd_r, tmp_r, AF.Exp, scale=-0.5)
        se_s2 = None
        if want_rms:
            ms_r = row_slot(6)
            C.vop("tensor_scalar_add", tmp_r, var_r, LN_EPS, waits=[("se", se_rstd)])
            C.vop("reciprocal", sum_r, tmp_r)
            C.vop("tensor_mul", ms_r, var_r, sum_r)
            ve_ms = C.vop("tensor_scalar_add", ms_r, ms_r, RMS_EPS)
            C.act(ms_r, ms_r, AF.Ln, waits=[("ve", ve_ms)])
            se_s2 = C.act(ms_r, ms_r, AF.Exp, scale=-0.5)
        del sum_r, sq_r
        C.mm(bank(2), ones_row(2), mean_r,
             waits=[("se", se_rstd)] + bar)
        te_b = C.mm(bank(3), ones_row(4), rstd_r)
        if want_rms:
            te_b = C.mm(bank(4), ones_row(6), row_slot(6),
                        waits=[("se", se_s2)])
        return te_b

    F_LN, F_TMP, F_H0, F_H1, F_SLAB = 0, 16384, 22528, 55296, 88064

    def load_wslab_a(wname, fs, buf_off, waits):
        tok = None
        for sp in range(4):
            src = tn[wname].ap().rearrange("(a p) f -> a p f", p=128)[
                4 * sp:4 * sp + 4, :, fs * 512:(fs + 1) * 512].rearrange(
                "a p f -> p a f")
            dst = sb_bf[:, buf_off // 2 + sp * 4 * 512:
                        buf_off // 2 + (sp + 1) * 4 * 512].rearrange(
                "p (a f) -> p a f", a=4)
            tok = C.dma("gq", dst, src, waits=waits if sp == 0 else ())
        return tok

    def load_wslab_b(wname, dt, buf_off, waits):
        tok = None
        for sp in range(4):
            src = tn[wname].ap().rearrange("(a p) d -> a p d", p=128)[
                8 * sp:8 * sp + 8, :, dt * 128:(dt + 1) * 128].rearrange(
                "a p d -> p a d")
            dst = sb_bf[:, buf_off // 2 + sp * 8 * 128:
                        buf_off // 2 + (sp + 1) * 8 * 128].rearrange(
                "p (a d) -> p a d", a=8)
            tok = C.dma("gq", dst, src, waits=waits if sp == 0 else ())
        return tok

    slab_guard = {"te": 0}
    wo_pref = {}

    def ffn_pass():
        ffn_pass.calls = getattr(ffn_pass, "calls", -1) + 1
        for ck in range(NCH):
            c0 = ck * CH
            ln_c = v_bf(F_LN, 16 * CH)
            h0c = v_bf(F_H0, 32 * CH)
            h1c = v_bf(F_H1, 32 * CH)
            t1 = v_f(F_TMP, CH)
            e0 = v_f(F_TMP + 2048, CH)
            e1 = v_f(F_TMP + 4096, CH)
            te_b = stats_ln(c0, F_H1, F_H0)
            ve_apply = []
            for dt in range(DT):
                xs = x_sb[:, dt * TPC + c0: dt * TPC + c0 + CH]
                C.vop("tensor_sub", t1, xs, bank(2), waits=[("te", te_b)])
                ve_apply.append(
                    C.vop("tensor_mul", ln_c[:, dt * CH:(dt + 1) * CH], t1, bank(3)))
            ve_ln = ve_apply[-1]
            for dt in range(DT):
                w = [("ve", ve_apply[dt]), ("gq", gq_consts)]
                C.mm(bank_row(0), tn["wg_sb"][:, 2 * dt:2 * dt + 1],
                     ln_c[:, dt * CH:(dt + 1) * CH], start=(dt == 0),
                     stop=(dt == DT - 1), waits=w, inc=(dt == DT - 1))
                C.mm(bank_row(1), tn["wg_sb"][:, 2 * dt + 1:2 * dt + 2],
                     ln_c[:, dt * CH:(dt + 1) * CH], start=(dt == 0),
                     stop=(dt == DT - 1), inc=(dt == DT - 1))
            te_g = C.counts["te"]
            g0_r, m_r = row_slot(0), row_slot(1)
            se_g = C.act(g0_r, bank_row(0), AF.Identity, waits=[("te", te_g)])
            C.vop("tensor_sub", m_r, g0_r, bank_row(1), waits=[("se", se_g)])
            ve_m = C.vop("tensor_single_scalar", m_r, m_r, 0.0, ALU.is_ge)
            te_mb = C.mm(bank(4), ones_row(8), m_r,
                         waits=[("ve", ve_m)])
            ve_m8 = C.vop("tensor_single_scalar", tn["mask_u8"][:, :], bank(4),
                          0.5, ALU.is_ge, waits=[("te", te_mb)])
            if DEBUG_TAP and ck == 0 and ffn_pass.calls == 0:
                C.dma("sync", tn["dbg"].ap()[0:1, 0:3584], rows_pool[0:1, :],
                      waits=[("ve", ve_m8)])
            SL = [F_SLAB, F_SLAB + 16384]
            ready = {}
            consumed = {}
            h_se = {}
            n_slab = 0
            for e in range(2):
                wname = "w0aT" if e == 0 else "w1aT"
                bcol = tn["b0a_col"] if e == 0 else tn["b1a_col"]
                func = AF.Gelu if e == 0 else AF.Relu
                hc = h0c if e == 0 else h1c
                for fs in range(8):
                    buf = n_slab % 2
                    if n_slab >= 2:
                        gw = [("te", consumed[n_slab - 2])]
                    else:
                        gw = [("te", slab_guard["te"])]
                    ready[n_slab] = load_wslab_a(wname, fs, SL[buf], gw)
                    slab = v_bf(SL[buf], 16 * CH)
                    for ft in range(4):
                        fg = fs * 4 + ft
                        bk = (n_slab * 4 + ft) % 2
                        w = [("gq", ready[n_slab]), ("ve", ve_m8)]
                        key = n_slab * 4 + ft
                        if key - 2 in h_se:
                            w.append(("se", h_se[key - 2]))
                        for kt in range(DT):
                            C.mm(bank(bk),
                                 slab[:, kt * 512 + ft * 128: kt * 512 + ft * 128 + 128],
                                 ln_c[:, kt * CH:(kt + 1) * CH],
                                 start=(kt == 0), stop=(kt == DT - 1),
                                 waits=w if kt == 0 else (), inc=(kt == DT - 1))
                        te_h = C.counts["te"]
                        h_se[key] = C.act(hc[:, fg * CH:(fg + 1) * CH], bank(bk),
                                          func, bias=bcol[:, fg:fg + 1],
                                          waits=[("te", te_h)])
                    consumed[n_slab] = C.counts["te"]
                    n_slab += 1
            h_done = (h_se[8 * 4 - 1], h_se[16 * 4 - 1])
            slab_guard["te"] = consumed[15]
            SLB = [F_SLAB, F_SLAB + 8192, F_SLAB + 16384, F_SLAB + 24576]
            bready = {}
            bconsumed = {}
            out_ve = []
            ev_se = []
            n_b = 0
            for dt in range(DT):
                for e in range(2):
                    wname = "w0bT" if e == 0 else "w1bT"
                    buf = n_b % 4
                    if n_b >= 4:
                        gw = [("te", bconsumed[n_b - 4])]
                    else:
                        gw = [("te", slab_guard["te"])]
                    bready[n_b] = load_wslab_b(wname, dt, SLB[buf], gw)
                    slabb = v_bf(SLB[buf], 32 * 128)
                    hc = h0c if e == 0 else h1c
                    bk = (dt % 2) * 2 + e
                    w = [("gq", bready[n_b]), ("se", h_done[e])]
                    if len(out_ve) >= 2:
                        w.append(("ve", out_ve[-2]))
                    for ft in range(FT):
                        C.mm(bank(bk), slabb[:, ft * 128:(ft + 1) * 128],
                             hc[:, ft * CH:(ft + 1) * CH],
                             start=(ft == 0), stop=(ft == FT - 1),
                             waits=w if ft == 0 else (), inc=(ft == FT - 1))
                    bconsumed[n_b] = C.counts["te"]
                    n_b += 1
                te_out = C.counts["te"]
                bk0, bk1 = (dt % 2) * 2, (dt % 2) * 2 + 1
                w = [("te", te_out)]
                if out_ve:
                    w.append(("ve", out_ve[-1]))
                C.act(e0, bank(bk0), AF.Identity, bias=tn["b0bh_col"][:, dt:dt + 1],
                      scale=0.5, waits=w)
                se_e = C.act(e1, bank(bk1), AF.Identity,
                             bias=tn["b1bh_col"][:, dt:dt + 1], scale=0.5)
                xs = x_sb[:, dt * TPC + c0: dt * TPC + c0 + CH]
                C.vop("select", t1, tn["mask_u8"][:, :], e0, e1,
                      waits=[("se", se_e), ("ve", ve_m8)])
                out_ve.append(C.vop("tensor_add", xs, xs, t1))
            slab_guard["te"] = bconsumed[n_b - 1]

    if _ph_on("ffn1"):
        ffn_pass()

    if _ph_on("proj"):
        P_XN, P_PBF, P_TMP, P_PSH, P_RT2, P_KRR = (
            0, 32768, 65536, 69632, 71680, 73728)
        kv_in = tn["kv_in"]
        xn = v_bf(P_XN, 16 * 1024)
        p_bf = v_bf(P_PBF, 16 * 1024)
        t1 = v_f(P_TMP, CH)
        t2 = v_f(P_TMP + 2048, CH)
        psh = v_bf(P_PSH, 1024)
        rt2 = v_bf(P_RT2, 1024)
        kr_ring = [v_bf(P_KRR + i * 2048, 1024) for i in range(2)]
        ve_ln2 = 0
        for ck in range(NCH):
            c0 = ck * CH
            te_b = stats_ln(c0, P_PBF, P_PBF + 16384, want_rms=True)
            for dt in range(DT):
                xs = x_sb[:, dt * TPC + c0: dt * TPC + c0 + CH]
                C.vop("tensor_sub", t1, xs, bank(2), waits=[("te", te_b)])
                C.vop("tensor_mul", t2, t1, bank(3))
                C.vop("tensor_add", xs, xs, t2)
                ve_ln2 = C.vop(
                    "scalar_tensor_tensor",
                    xn[:, dt * 1024 + c0: dt * 1024 + c0 + CH],
                    t2, tn["rms_col"][:, dt:dt + 1], bank(4),
                    ALU.mult, ALU.mult)
        SL = [F_SLAB, F_SLAB + 16384]
        ready = {}
        consumed = {}
        wq_se = {}
        ng = 0
        for fs in range(4):
            buf = fs % 2
            gw = ([("te", consumed[fs - 2])] if fs >= 2
                  else [("te", slab_guard["te"])])
            ready[fs] = load_wslab_a("wqT", fs, SL[buf], gw)
            slab = v_bf(SL[buf], 16 * CH)
            for ft in range(4):
                fg = fs * 4 + ft
                for ck in range(NCH):
                    c0 = ck * CH
                    bk = 4 + (ng % 2)
                    w = [("gq", ready[fs]), ("ve", ve_ln2)]
                    if ng - 2 in wq_se:
                        w.append(("se", wq_se[ng - 2]))
                    for kt in range(DT):
                        C.mm(bank(bk),
                             slab[:, kt * 512 + ft * 128: kt * 512 + ft * 128 + 128],
                             xn[:, kt * 1024 + c0: kt * 1024 + c0 + CH],
                             start=(kt == 0), stop=(kt == DT - 1),
                             waits=w if kt == 0 else (), inc=(kt == DT - 1))
                    te_w = C.counts["te"]
                    wq_se[ng] = C.act(
                        p_bf[:, fg * 1024 + c0: fg * 1024 + c0 + CH], bank(bk),
                        AF.Identity, bias=tn["bq_col"][:, fg:fg + 1],
                        waits=[("te", te_w)])
                    ng += 1
            consumed[fs] = C.counts["te"]
        slab_guard["te"] = consumed[3]
        se_wq_of = {}
        for h in range(H):
            se_wq_of[h] = wq_se[(h // 4) * 8 + (h % 4) * 2 + 1]
        se_wq = wq_se[ng - 1]
        cosb, sinb = tn["cos_bf"], tn["sin_bf"]
        kv_stage = {}
        for h in range(H):
            krh = kr_ring[h % 2]
            w = [("se", se_wq_of[h])]
            if h >= 2:
                w.append(("sq", kv_stage[h - 2]))
            C.vop("tensor_copy", psh[0:64, :],
                  p_bf[64:128, h * 1024:(h + 1) * 1024], waits=w)
            C.vop("tensor_copy", psh[64:128, :],
                  p_bf[0:64, h * 1024:(h + 1) * 1024])
            C.vop("tensor_mul", krh, p_bf[:, h * 1024:(h + 1) * 1024],
                  cosb[:, :])
            C.vop("tensor_mul", rt2, psh, sinb[:, :])
            ve_kr = C.vop("tensor_add", krh, krh, rt2)
            kv_stage[h] = C.dma(
                "sync", kv_in.ap()[128 * h: 128 * (h + 1), :], krh,
                waits=[("ve", ve_kr)])
        kr_staged = kv_stage[H - 1]
        C.coll("AllGather", [kv_in.ap()[0:2048, :].opt()],
               [tn["kr_gout"].ap().opt()], waits=[("sq", kr_staged)])
        P_VST = 77824
        vstage = [v_bf(P_VST + i * 4096, 2048) for i in range(2)]
        kv_in_flat = kv_in.ap().rearrange("r c -> (r c)")
        vt_grp_se = {}
        vt_grp_te = {}
        dma_vst = {}
        v_staged = 0
        gidx = 0
        for tt in range(8):
            for sub in range(2):
                bko = (gidx % 2) * 1024
                for i in range(8):
                    h = sub * 8 + i
                    w = [("se", se_wq_of[h])]
                    if gidx >= 2 and i == 0:
                        w.append(("se", vt_grp_se[gidx - 2]))
                    C.transpose(
                        tn["psT"][:, bko + i * 128: bko + (i + 1) * 128],
                        p_bf[:, h * 1024 + tt * 128: h * 1024 + tt * 128 + 128],
                        waits=w, inc=(i == 7))
                vt_grp_te[gidx] = C.counts["te"]
                for i in range(8):
                    h = sub * 8 + i
                    w = [("te", vt_grp_te[gidx])] if i == 0 else []
                    if tt >= 2 and i == 0:
                        w.append(("sq", dma_vst[tt - 2]))
                    vt_grp_se[gidx] = C.act(
                        vstage[tt % 2][:, h * 128:(h + 1) * 128],
                        tn["psT"][:, bko + i * 128: bko + (i + 1) * 128],
                        AF.Copy, waits=w)
                gidx += 1
            dst = kv_in_flat.rearrange("(half t d) -> half t d",
                                       half=2, t=1024, d=2048)[
                1, tt * 128:(tt + 1) * 128, :]
            dma_vst[tt] = C.dma("sync", dst, vstage[tt % 2][:, :],
                                waits=[("se", vt_grp_se[gidx - 1])])
            v_staged = dma_vst[tt]
        C.coll("AllGather", [kv_in.ap()[2048:4096, :].opt()],
               [tn["v_gout"].ap().opt()], waits=[("sq", v_staged)])

    def attn_phase():
        A_KR, A_V, A_A, A_OST, A_SQ, A_RB = (
            0, 32768, 65536, 73728, 77824, 81920)
        kr_sb = v_bf(A_KR, 2 * S)
        v_sb = v_bf(A_V, 2 * S)
        a_ring = [v_bf(A_A + i * 2048, 1024) for i in range(2)]
        o_stage = [v_bf(A_OST + i * 1024, CH) for i in range(2)]
        sq_scr = v_bf(A_SQ, 2048)
        rb_sb = [v_f(A_RB + i * 2048, CH) for i in range(2)]
        o_in = tn["o_in"]
        bar = C.barrier()
        C.gwait("sync", bar)
        C.gwait("gpsimd", bar)
        if ATTN_UPTO == "gather":
            return
        gq_kr = 0
        for jh in range(2):
            dst = kr_sb[:, jh * S: (jh + 1) * S].rearrange("p (r t) -> p r t", r=8)
            gq_kr = C.gdma_off(
                dst, tn["kr_gout"], jh,
                [[1024, 128], [2048 * 1024, 8], [1, 1024]],
                waits=[("cc", C.counts["cc"] - 1), ("sq", sq_consts)])
        for h in range(HPC):
            for cc_ in range(4):
                C.act(sq_scr, kr_sb[:, h * S + cc_ * 2048: h * S + (cc_ + 1) * 2048],
                      AF.Square,
                      waits=([("gq", gq_kr)] + bar) if (cc_ == 0 and h == 0) else ())
                se_sq = C.counts["se"]
                for k4 in range(4):
                    C.mm(bank_row(0), tn["ones128_bf"][:, :],
                         sq_scr[:, k4 * 512:(k4 + 1) * 512],
                         waits=[("se", se_sq), ("ve", C.counts["ve"])] + (
                             bar if (cc_ == 0 and h == 0 and k4 == 0) else []))
                    te_n = C.counts["te"]
                    C.vop("reduce_max",
                          row_slot(1)[0:1, 16 * h + 4 * cc_ + k4:
                                      16 * h + 4 * cc_ + k4 + 1],
                          bank_row(0), axis=AX.X, waits=[("te", te_n)])
        C.vop("reduce_max", row_slot(2)[0:1, 0:1], row_slot(1)[0:1, 0:32],
              axis=AX.X)
        ve_c = C.vop("tensor_scalar_mul", row_slot(2)[0:1, 0:1],
                     row_slot(2)[0:1, 0:1], -INV_SQRT_DK)
        te_c = C.mm(bank(0, 128, 1), ones_row(2),
                    row_slot(2)[0:1, 0:1], waits=[("ve", ve_c)])
        se_negc = C.act(tn["negC_col"][:, 0:1], bank(0, 128, 1), AF.Copy,
                        waits=[("te", te_c)])
        gq_kv = 0
        for jh in range(2):
            dst = v_sb[:, jh * S: (jh + 1) * S]
            gq_kv = C.gdma_off(
                dst, tn["v_gout"], 2 + jh,
                [[2048, 128], [128 * 2048, 64], [1, 128]],
                waits=[("cc", C.counts["cc"])] if jh == 0 else ())
        se_vt = 0
        if ATTN_UPTO in ("loads", "vt", "cbound"):
            C.gwait("sync", [("gq", gq_kv), ("se", se_negc)])
            return
        ve_precalc = C.counts["ve"]
        if ATTN_UPTO == "cbound":
            C.gwait("sync", [("se", C.counts["se"]), ("ve", ve_precalc)])
            return
        s_guard_se = [0, 0]
        a_guard_te = [0, 0]
        o_guard_ve = [0, 0]
        rs_guard_se = [0, 0]
        rs_row_guard_ve = [0, 0]
        rec_row_guard_te = [0, 0]
        ost_guard_sq = [0, 0]
        rb_guard_ve = [0, 0]
        pend_epi = [None]
        first_s = True
        for qc in range(16):
            E = 4 * qc + 4
            exp_toks = {}
            s_toks = {}
            av_w = {}

            def emit_avrs(kt):
                for h in range(HPC):
                    pair = kt % 2
                    w = list(av_w[kt]) if h == 0 else []
                    if kt == 0:
                        w += [("ve", o_guard_ve[h]), ("se", rs_guard_se[h])]
                    C.mm(bank(2 + h),
                         v_sb[:, (h * 64 + kt) * 128:(h * 64 + kt + 1) * 128],
                         a_ring[pair][:, h * CH:(h + 1) * CH],
                         start=(kt == 0), stop=(kt == E - 1), waits=w)
                    a_guard_te[pair] = C.mm(
                        bank_row(4 + h), tn["ones128_bf"][:, :],
                        a_ring[pair][:, h * CH:(h + 1) * CH],
                        start=(kt == 0), stop=(kt == E - 1))

            for kt in range(E):
                for h in range(HPC):
                    w = [("se", se_negc), ("se", s_guard_se[h])]
                    if first_s:
                        w += [("gq", gq_kv), ("ve", C.counts["ve"])]
                        first_s = False
                    s_toks[(kt, h)] = C.mm(
                        bank(h),
                        kr_sb[:, h * S + kt * 128: h * S + kt * 128 + 128],
                        kr_sb[:, h * S + qc * 512: h * S + qc * 512 + 512],
                        waits=w)
                if kt == 1 and pend_epi[0] is not None:
                    pend_epi[0]()
                    pend_epi[0] = None
                pair = kt % 2
                for h in range(HPC):
                    w = [("te", s_toks[(kt, h)]), ("te", a_guard_te[pair])]
                    exp_toks[(kt, h)] = C.act(
                        a_ring[pair][:, h * CH:(h + 1) * CH], bank(h), AF.Exp,
                        bias=tn["negC_col"][:, 0:1], scale=INV_SQRT_DK, waits=w)
                    s_guard_se[h] = exp_toks[(kt, h)]
                exp_toks[kt] = exp_toks[(kt, 1)]
                if kt >= E - 4:
                    j = kt - 4 * qc
                    for h in range(HPC):
                        ve_mask = C.vop(
                            "tensor_mul", a_ring[pair][:, h * CH:(h + 1) * CH],
                            a_ring[pair][:, h * CH:(h + 1) * CH],
                            tn["masks_sb"][:, j * 512:(j + 1) * 512],
                            waits=[("se", exp_toks[kt])] if h == 0 else ())
                    av_w[kt] = [("ve", ve_mask)]
                else:
                    av_w[kt] = [("se", exp_toks[kt])]
                if kt >= 1:
                    emit_avrs(kt - 1)
            emit_avrs(E - 1)
            te_avdone = C.counts["te"]
            ve_recs = [0, 0]
            for h in range(HPC):
                rs_r = row_slot(3 + h)
                rec_r = row_slot(5 + h)
                se_rs = C.act(rs_r, bank_row(4 + h), AF.Identity,
                              waits=[("te", te_avdone),
                                     ("ve", rs_row_guard_ve[h])])
                rs_row_guard_ve[h] = ve_recs[h] = C.vop(
                    "reciprocal", rec_r, rs_r,
                    waits=[("se", se_rs), ("te", rec_row_guard_te[h])])

            def make_epi(qc=qc, ve_recs=tuple(ve_recs)):
                def epi():
                    for h in range(HPC):
                        rec_r = row_slot(5 + h)
                        te_bc = C.mm(bank(4 + h), ones_row(5 + h), rec_r,
                                     waits=[("ve", ve_recs[h])])
                        rec_row_guard_te[h] = te_bc
                        se_cp = C.act(rb_sb[h], bank(4 + h), AF.Copy,
                                      waits=[("te", te_bc),
                                             ("ve", rb_guard_ve[h])])
                        rs_guard_se[h] = se_cp
                        ve_o = C.vop("tensor_mul", o_stage[h], bank(2 + h),
                                     rb_sb[h],
                                     waits=[("se", se_cp),
                                            ("sq", ost_guard_sq[h])])
                        o_guard_ve[h] = ve_o
                        rb_guard_ve[h] = ve_o
                        ost_guard_sq[h] = C.dma(
                            "sync",
                            o_in.ap()[128 * h: 128 * (h + 1),
                                      qc * 512: qc * 512 + 512],
                            o_stage[h], waits=[("ve", ve_o)])
                return epi

            pend_epi[0] = make_epi()
        if pend_epi[0] is not None:
            pend_epi[0]()
            pend_epi[0] = None
        o_dma_last = max(ost_guard_sq)
        if ATTN_UPTO == "main":
            C.gwait("sync", [("sq", o_dma_last)])
            return
        if _ph_on("wo"):
            wo_pref[0] = load_wslab_a("woT", 0, F_SLAB,
                                      [("te", slab_guard["te"])])
            wo_pref[1] = load_wslab_a("woT", 1, F_SLAB + 16384, [])
        C.coll("AllGather", [o_in.ap().opt()], [tn["o_gout"].ap().opt()],
               waits=[("sq", o_dma_last)])

    if _ph_on("attn"):
        attn_phase()

    if _ph_on("wo"):
        W_OFM, W_TMP = 0, 32768
        o_fm = v_bf(W_OFM, 16 * 1024)
        e0 = v_f(W_TMP, CH)
        bar = C.barrier()
        C.gwait("sync", bar)
        C.gwait("gpsimd", bar)
        sq_ofm = 0
        for gh in range(2):
            dst = o_fm[:, gh * 8192:(gh + 1) * 8192].rearrange(
                "p (g t) -> p g t", g=8)
            sq_ofm = C.gdma_off(dst, tn["o_gout"], 10 + gh,
                                [[8192, 128], [128 * 8192, 8], [1, 1024]],
                                waits=[("cc", C.counts["cc"])])
        SL = [F_SLAB, F_SLAB + 16384]
        consumed = {}
        wo_ve = []
        ng = 0
        for fs in range(4):
            buf = fs % 2
            if fs in wo_pref:
                ready = wo_pref[fs]
            else:
                gw = ([("te", consumed[fs - 2])] if fs >= 2
                      else [("te", slab_guard["te"])])
                ready = load_wslab_a("woT", fs, SL[buf], gw)
            slab = v_bf(SL[buf], 16 * CH)
            for ft in range(4):
                fg = fs * 4 + ft
                for ck in range(NCH):
                    c0 = ck * CH
                    bk = 4 + (ng % 2)
                    w = [("gq", ready), ("gq", sq_ofm)]
                    if ng == 0:
                        w += bar
                    if len(wo_ve) >= 2:
                        w.append(("ve", wo_ve[-2]))
                    for kt in range(DT):
                        C.mm(bank(bk),
                             slab[:, kt * 512 + ft * 128: kt * 512 + ft * 128 + 128],
                             o_fm[:, kt * 1024 + c0: kt * 1024 + c0 + CH],
                             start=(kt == 0), stop=(kt == DT - 1),
                             waits=w if kt == 0 else (), inc=(kt == DT - 1))
                    te_w = C.counts["te"]
                    wse = [("te", te_w)]
                    if wo_ve:
                        wse.append(("ve", wo_ve[-1]))
                    se_e = C.act(e0, bank(bk), AF.Identity,
                                 bias=tn["bo_col"][:, fg:fg + 1], waits=wse)
                    xs = x_sb[:, fg * TPC + c0: fg * TPC + c0 + CH]
                    wo_ve.append(C.vop("tensor_add", xs, xs, e0,
                                       waits=[("se", se_e)]))
                    ng += 1
            consumed[fs] = C.counts["te"]
        slab_guard["te"] = consumed[3]

    if _ph_on("ffn2"):
        ffn_pass()

    bar = C.barrier()
    st = 0
    for dt in range(DT):
        st = C.dma("sync", tn["out"].ap()[dt * 128:(dt + 1) * 128, :],
                   x_sb[:, dt * TPC:(dt + 1) * TPC], waits=bar if dt == 0 else ())
    C.gwait("sync", [("sq", st)])
    C.gwait("gpsimd", [("sq", st)])



def build():
    nc = bass.Bass("TRN2", target_bir_lowering=False)
    tn = {}

    tn["xT"] = nc.dram_tensor("xT", [D, TPC], F32, kind="ExternalInput").ap()
    tn["out"] = nc.dram_tensor("out_y", [D, TPC], F32, kind="ExternalOutput")
    tn["dbg"] = nc.dram_tensor("dbg", [128, 4096], F32, kind="ExternalOutput")
    tn["wqT"] = nc.dram_tensor("wqT", [D, D], F32, kind="ExternalInput")
    tn["woT"] = nc.dram_tensor("woT", [D, D], F32, kind="ExternalInput")
    tn["w0aT"] = nc.dram_tensor("w0aT", [D, FF], F32, kind="ExternalInput")
    tn["w0bT"] = nc.dram_tensor("w0bT", [FF, D], F32, kind="ExternalInput")
    tn["w1aT"] = nc.dram_tensor("w1aT", [D, FF], F32, kind="ExternalInput")
    tn["w1bT"] = nc.dram_tensor("w1bT", [FF, D], F32, kind="ExternalInput")
    tn["wg_cols"] = nc.dram_tensor("wg_cols", [128, 32], F32, kind="ExternalInput")
    for name, k in [("bq_col", 16), ("bo_col", 16), ("b0a_col", 32),
                    ("b1a_col", 32), ("b0b_col", 16), ("b1b_col", 16),
                    ("rms_col", 16)]:
        tn["dram_" + name] = nc.dram_tensor(name, [128, k], F32,
                                            kind="ExternalInput")
    tn["dram_pos_sb"] = nc.dram_tensor("pos", [1, TPC], F32, kind="ExternalInput")

    tn["dram_ones1_f"] = nc.inline_tensor(np.ones((1, 128), np.float32), "c_ones1")
    tn["dram_i64_sb"] = nc.inline_tensor(
        np.arange(64, dtype=np.float32).reshape(1, 64), "c_i64")
    tn["dram_eps_col"] = nc.inline_tensor(
        np.full((1, 1), LN_EPS, np.float32), "c_eps")
    tn["dram_ones128_bf"] = nc.inline_tensor(
        np.ones((128, 1), ml_dtypes.bfloat16), "c_ones128")
    tn["dram_eye_bf"] = nc.inline_tensor(
        np.eye(128, dtype=ml_dtypes.bfloat16), "c_eye")
    masks = np.zeros((128, 4 * 512), np.float32)
    for j in range(4):
        k_idx = np.arange(128)[:, None]
        q_idx = np.arange(512)[None, :]
        masks[:, j * 512:(j + 1) * 512] = (j * 128 + k_idx <= q_idx)
    tn["dram_masks"] = nc.inline_tensor(masks.astype(ml_dtypes.bfloat16), "c_masks")

    tn["kv_in"] = nc.dram_tensor("kv_in", [4096, 1024], BF16)
    tn["kr_gout"] = nc.dram_tensor("kr_gout", [8 * 2048, 1024], BF16,
                                   addr_space="Shared")
    tn["v_gout"] = nc.dram_tensor("v_gout", [8 * 2048, 1024], BF16,
                                  addr_space="Shared")
    tn["o_in"] = nc.dram_tensor("o_in", [256, 8192], BF16)
    tn["o_gout"] = nc.dram_tensor("o_gout", [2048, 8192], BF16,
                                  addr_space="Shared")
    tn["dram_meta"] = nc.dram_tensor("meta", [1, 64], mybir.dt.uint32,
                                     kind="ExternalInput")

    with ExitStack() as ctx:
        e = ctx.enter_context
        tn["x_sb"] = e(nc.sbuf_tensor("x_sb", [128, DT * TPC], F32))
        tn["masks_sb"] = e(nc.sbuf_tensor("masks_sb", [128, 2048], BF16))
        tn["ones128_bf"] = e(nc.sbuf_tensor("ones128_bf", [128, 1], BF16))
        tn["eye_sb"] = e(nc.sbuf_tensor("eye_sb", [128, 128], BF16))
        tn["ones1_f"] = e(nc.sbuf_tensor("ones1_f", [1, 128], F32))
        for name, k in [("bq_col", 16), ("bo_col", 16), ("b0a_col", 32),
                        ("b1a_col", 32), ("b0b_col", 16), ("b1b_col", 16),
                        ("rms_col", 16), ("b0bh_col", 16), ("b1bh_col", 16)]:
            tn[name] = e(nc.sbuf_tensor(name + "_sb", [128, k], F32))
        tn["wg_sb"] = e(nc.sbuf_tensor("wg_sb", [128, 32], BF16))
        tn["i64_sb"] = e(nc.sbuf_tensor("i64_sb", [1, 64], F32))
        tn["invf_sb"] = e(nc.sbuf_tensor("invf_sb", [1, 64], F32))
        tn["rows_pool"] = e(nc.sbuf_tensor("rows_pool", [1, 3584], F32))
        tn["cos_bf"] = e(nc.sbuf_tensor("cos_bf", [128, TPC], BF16))
        tn["sin_bf"] = e(nc.sbuf_tensor("sin_bf", [128, TPC], BF16))
        tn["negC_col"] = e(nc.sbuf_tensor("negC_col", [128, 2], F32))
        tn["mask_u8"] = e(nc.sbuf_tensor("mask_u8", [128, 512], U8))
        tn["eps_col"] = e(nc.sbuf_tensor("eps_col", [1, 1], F32))
        tn["meta_sb"] = e(nc.sbuf_tensor("meta_sb", [1, 64], mybir.dt.uint32))
        SCR = 120832
        tn["scratch"] = e(nc.sbuf_tensor("scratch", [128, SCR], U8))
        tn["psA"] = e(nc.psum_tensor("psA", [128, 3072], F32))
        tn["psT"] = e(nc.psum_tensor("psT", [128, 2048], BF16))

        sems = {}
        for s_ in ENGS:
            sems[s_] = e(nc.semaphore("sem_" + s_))
        block = e(nc.Block())

        def run(eng_name):
            def f(eng):
                C = Ctx(nc, eng_name, eng, sems, tn)
                if eng_name == "gpsimd":
                    with eng.register("roff") as roff:
                        C.roff = roff
                        emit_program(C)
                else:
                    emit_program(C)
            return f

        block.gpsimd(run("gpsimd"))
        block.scalar(run("scalar"))
        block.tensor(run("tensor"))
        block.vector(run("vector"))
        block.sync(run("sync"))

    return nc


_CACHED = {}


def kernel(**inputs):
    X = np.asarray(inputs["L0_attn_X"])
    asc = np.ascontiguousarray

    def colpack(v, k):
        return asc(np.asarray(v, np.float32).reshape(k, 128).T)

    shared = {
        "wqT": asc(np.asarray(inputs["L0_attn_Wq_w"], np.float32).T),
        "woT": asc(np.asarray(inputs["L0_attn_Wo_w"], np.float32).T),
        "w0aT": asc(np.asarray(inputs["L0_ffn_W0a"], np.float32).T),
        "w0bT": asc(np.asarray(inputs["L0_ffn_W0b"], np.float32).T),
        "w1aT": asc(np.asarray(inputs["L0_ffn_W1a"], np.float32).T),
        "w1bT": asc(np.asarray(inputs["L0_ffn_W1b"], np.float32).T),
        "wg_cols": asc(np.asarray(inputs["L0_ffn_Wg"], np.float32).T
                       .reshape(16, 128, 2).transpose(1, 0, 2).reshape(128, 32)),
        "bq_col": colpack(inputs["L0_attn_Wq_b"], 16),
        "bo_col": colpack(inputs["L0_attn_Wo_b"], 16),
        "b0a_col": colpack(inputs["L0_ffn_b0a"], 32),
        "b1a_col": colpack(inputs["L0_ffn_b1a"], 32),
        "b0b_col": colpack(inputs["L0_ffn_b0b"], 16),
        "b1b_col": colpack(inputs["L0_ffn_b1b"], 16),
        "rms_col": colpack(inputs["L0_attn_rms_w"], 16),
    }
    in_maps = []
    for c in range(NCORE):
        m = dict(shared)
        m["xT"] = asc(X[0, c * TPC:(c + 1) * TPC, :].T)
        m["pos"] = np.arange(c * TPC, (c + 1) * TPC,
                             dtype=np.float32).reshape(1, TPC)
        meta = np.zeros(64, np.uint32)
        for jh in range(2):
            meta[jh] = (256 * c + 128 * jh) * 1024
            meta[2 + jh] = 256 * c + 128 * jh
        for gh in range(2):
            meta[10 + gh] = 1024 * c + gh * 8 * 128 * 8192
        m["meta"] = meta.reshape(1, 64)
        in_maps.append(m)

    if "nc" not in _CACHED:
        _CACHED["nc"] = build()
    res = run_bass_kernel_spmd(_CACHED["nc"], in_maps, core_ids=list(range(NCORE)))
    y = np.empty((S, D), np.float32)
    for c in range(NCORE):
        y[c * TPC:(c + 1) * TPC, :] = res.results[c]["out_y"].T
    return y.reshape(1, S, D)


# version 06
# speedup vs baseline: 1.0420x; 1.0118x over previous
import math
from contextlib import ExitStack

import numpy as np
import ml_dtypes

import concourse.bass as bass
import concourse.mybir as mybir
from concourse.bass_utils import run_bass_kernel_spmd

F32 = mybir.dt.float32
BF16 = mybir.dt.bfloat16
U8 = mybir.dt.uint8
AF = mybir.ActivationFunctionType
ALU = mybir.AluOpType
AX = mybir.AxisListType

S = 8192
D = 2048
TPC = 1024
NCORE = 8
H = 16
DK = 128
HPC = 2
DT = 16
FF = 4096
FT = 32
CH = 512
NCH = 2
INV_SQRT_DK = 1.0 / math.sqrt(DK)
LN_EPS = 1e-5
RMS_EPS = float(np.finfo(np.float32).eps)

TWO_PI = 2.0 * math.pi
CW_HI = 6.28125
CW_MID = float(np.float32(TWO_PI - CW_HI))
CW_LO = float(np.float32(TWO_PI - CW_HI - float(np.float32(TWO_PI - CW_HI))))
RND_C = float(np.float32(2.0 ** 23))
HALF_PI = float(np.float32(math.pi / 2.0))

DEBUG_TAP = False
ATTN_UPTO = "all"
BUILD_UPTO = "ffn2"
_PHASE_ORDER = ["tables", "ffn1", "proj", "attn", "wo", "ffn2"]

ENGS = ("te", "se", "ve", "sq", "gq", "cc")


def _ph_on(name):
    return (_PHASE_ORDER.index(name) <= _PHASE_ORDER.index(BUILD_UPTO))


class Ctx:

    def __init__(self, nc, eng_name, eng, sems, tn):
        self.nc = nc
        self.eng_name = eng_name
        self.eng = eng
        self.sems = sems
        self.tn = tn
        self.counts = {k: 0 for k in sems}
        self.last_wait = {k: 0 for k in sems}

    def on(self, name):
        return self.eng_name == name

    def _waits(self, waits):
        for s, v in waits:
            if v is None or v <= 0:
                continue
            if v > self.counts[s]:
                raise RuntimeError(f"wait on future {s}: {v} > {self.counts[s]}")
            if v > self.last_wait[s]:
                self.eng.wait_ge(self.sems[s], v)
                self.last_wait[s] = v

    def bump(self, sem, n):
        self.counts[sem] += n
        return self.counts[sem]

    def barrier(self):
        return [(s, self.counts[s]) for s in ("te", "se", "ve", "sq", "gq", "cc")]

    def dma(self, queue, out, in_, waits=(), **kw):
        sem = "sq" if queue == "sync" else "gq"
        if self.on("sync" if queue == "sync" else "gpsimd"):
            self._waits(waits)
            self.eng.dma_start(out=out, in_=in_, **kw).then_inc(self.sems[sem], 16)
        return self.bump(sem, 16)

    def mm(self, out, lhsT, rhs, start=True, stop=True, waits=(), inc=True):
        if self.on("tensor"):
            self._waits(waits)
            i = self.eng.matmul(out, lhsT, rhs, start=start, stop=stop)
            if inc:
                i.then_inc(self.sems["te"], 1)
        return self.bump("te", 1) if inc else None

    def transpose(self, out, in_, waits=(), inc=True):
        if self.on("tensor"):
            self._waits(waits)
            i = self.eng.transpose(out, in_, self.tn["eye_sb"][:, :])
            if inc:
                i.then_inc(self.sems["te"], 1)
        return self.bump("te", 1) if inc else None

    def act(self, out, in_, func, bias=0.0, scale=1.0, accum_out=None, waits=(),
            inc=True):
        if self.on("scalar"):
            self._waits(waits)
            i = self.eng.activation(out, in_, func, bias=bias, scale=scale,
                                    accum_out=accum_out)
            if inc:
                i.then_inc(self.sems["se"], 1)
        return self.bump("se", 1) if inc else None

    def vop(self, fn, *args, waits=(), inc=True, **kw):
        if self.on("vector"):
            self._waits(waits)
            i = getattr(self.eng, fn)(*args, **kw)
            if inc:
                i.then_inc(self.sems["ve"], 1)
        return self.bump("ve", 1) if inc else None

    def coll(self, kind, ins, outs, waits=()):
        if self.on("gpsimd"):
            self._waits(waits)
            self.eng.collective_compute(
                kind, ALU.bypass, replica_groups=[list(range(NCORE))],
                ins=ins, outs=outs,
            ).then_inc(self.sems["cc"])
        return self.bump("cc", 1)

    def gdma_off(self, out, dram_tensor, meta_idx, pattern, waits=()):
        if self.on("gpsimd"):
            self._waits(waits)
            self.eng.reg_load(self.roff, self.tn["meta_sb"][0:1,
                                                            meta_idx:meta_idx + 1])
            src = bass.AP(dram_tensor, self.roff, pattern)
            self.eng.dma_start(out=out, in_=src).then_inc(self.sems["gq"], 16)
        return self.bump("gq", 16)

    def gwait(self, eng_name, waits):
        if self.on(eng_name):
            self._waits(waits)

    def all_barrier(self):
        b = self.barrier()
        for e in ("tensor", "scalar", "vector", "sync", "gpsimd"):
            self.gwait(e, b)



def emit_program(C: Ctx):
    tn = C.tn
    x_sb = tn["x_sb"]
    scr = tn["scratch"]
    sb_bf = scr[:, :].bitcast(BF16)
    sb_f = scr[:, :].bitcast(F32)
    ps = tn["psA"]

    def bank(i, rows=128, cols=512):
        return ps[0:rows, i * 512:i * 512 + cols]

    def bank_row(i, cols=512):
        return ps[0:1, i * 512:i * 512 + cols]

    rows_pool = tn["rows_pool"]

    def row_slot(i, cols=512):
        return rows_pool[0:1, i * 512: i * 512 + cols]

    def ones_row(i):
        return tn["ones1_f"][:, :]

    def v_bf(off_b, cols):
        assert off_b % 2 == 0
        return sb_bf[:, off_b // 2: off_b // 2 + cols]

    def v_f(off_b, cols, p0=0, p1=128):
        assert off_b % 4 == 0
        return sb_f[p0:p1, off_b // 4: off_b // 4 + cols]

    pos_sb = v_f(16384, TPC, 0, 1)
    for name in ["bq_col", "bo_col", "b0a_col", "b1a_col", "b0b_col", "b1b_col",
                 "rms_col", "ones1_f", "i64_sb", "eps_col"]:
        C.dma("sync", tn[name][:, :], tn["dram_" + name].ap())
    C.dma("sync", pos_sb, tn["dram_pos_sb"].ap())
    C.dma("sync", tn["masks_sb"][:, :], tn["dram_masks"].ap())
    C.dma("sync", tn["meta_sb"][:, :], tn["dram_meta"].ap())
    C.dma("sync", tn["ones128_bf"][:, :], tn["dram_ones128_bf"].ap())
    C.dma("sync", tn["eye_sb"][:, :], tn["dram_eye_bf"].ap())
    C.dma("gq", tn["wg_sb"][:, :], tn["wg_cols"].ap())
    for dt in range(DT):
        C.dma("sync", x_sb[:, dt * TPC:(dt + 1) * TPC],
              tn["xT"][dt * 128:(dt + 1) * 128, :])
    sq_consts = C.counts["sq"]
    gq_consts = C.counts["gq"]

    C.vop("tensor_scalar_mul", tn["b0bh_col"][:, :], tn["b0b_col"][:, :], 0.5,
          waits=[("sq", sq_consts)])
    C.vop("tensor_scalar_mul", tn["b1bh_col"][:, :], tn["b1b_col"][:, :], 0.5)

    if _ph_on("tables"):
        C.act(tn["invf_sb"][:, :], tn["i64_sb"][:, :], AF.Exp,
              scale=-math.log(10000.0) / 64.0, waits=[("sq", sq_consts)])
        se_invf = C.counts["se"]
        for cc_ in range(2):
            C.mm(ps[0:64, cc_ * 512:cc_ * 512 + 512], tn["invf_sb"][:, :],
                 pos_sb[0:1, cc_ * 512:(cc_ + 1) * 512],
                 waits=[("se", se_invf)])
        te_ang = C.counts["te"]
        ang = v_f(0, 1024, 0, 64)
        u = v_f(4096, 1024, 0, 64)
        kk = v_f(8192, 1024, 0, 64)
        rr = v_f(12288, 1024, 0, 64)
        C.vop("tensor_copy", ang, ps[0:64, 0:1024], waits=[("te", te_ang)])
        for trig, dst in (("sin", tn["sin_bf"]), ("cos", tn["cos_bf"])):
            off = 0.25 if trig == "cos" else 0.0
            C.vop("tensor_scalar", u, ang, 1.0 / TWO_PI, off, ALU.mult, ALU.add)
            C.vop("tensor_scalar", kk, u, RND_C, RND_C, ALU.add, ALU.subtract)
            C.vop("scalar_tensor_tensor", rr, kk, -CW_HI, ang, ALU.mult, ALU.add)
            C.vop("scalar_tensor_tensor", rr, kk, -CW_MID, rr, ALU.mult, ALU.add)
            ve_red = C.vop("scalar_tensor_tensor", rr, kk, -CW_LO, rr, ALU.mult,
                           ALU.add)
            if trig == "cos":
                ve_red = C.vop("tensor_scalar_add", rr, rr, HALF_PI)
            se_sin = C.act(dst[0:64, :], rr, AF.Sin, waits=[("ve", ve_red)])
            C.vop("tensor_copy", dst[64:128, :], dst[0:64, :],
                  waits=[("se", se_sin)])
            if trig == "sin":
                C.vop("tensor_scalar_mul", dst[0:64, :], dst[0:64, :], -1.0)

    def stats_ln(c0, xb_off, x2_off, want_rms=False):
        bar = C.barrier()
        xb = v_bf(xb_off, 16 * CH)
        x2 = v_bf(x2_off, 16 * CH)
        se_toks = []
        for dt in range(DT):
            src = x_sb[:, dt * TPC + c0: dt * TPC + c0 + CH]
            C.act(xb[:, dt * CH:(dt + 1) * CH], src, AF.Copy,
                  waits=bar if dt == 0 else ())
            se_toks.append(C.act(x2[:, dt * CH:(dt + 1) * CH], src, AF.Square))
        for dt in range(DT):
            w = [("se", se_toks[dt])]
            if dt == 0:
                w += bar
            C.mm(bank_row(0), tn["ones128_bf"][:, :], xb[:, dt * CH:(dt + 1) * CH],
                 start=(dt == 0), stop=(dt == DT - 1), waits=w, inc=(dt == DT - 1))
            C.mm(bank_row(1), tn["ones128_bf"][:, :], x2[:, dt * CH:(dt + 1) * CH],
                 start=(dt == 0), stop=(dt == DT - 1), waits=w, inc=(dt == DT - 1))
        te_sums = C.counts["te"]
        sum_r, sq_r, mean_r = row_slot(0), row_slot(1), row_slot(2)
        var_r, rstd_r, tmp_r = row_slot(3), row_slot(4), row_slot(5)
        C.act(sum_r, bank_row(0), AF.Identity, waits=[("te", te_sums)] + bar)
        se_evac = C.act(sq_r, bank_row(1), AF.Identity)
        C.vop("tensor_scalar_mul", mean_r, sum_r, 1.0 / D,
              waits=[("se", se_evac)] + bar)
        C.vop("tensor_mul", tmp_r, mean_r, mean_r)
        ve_var = C.vop("scalar_tensor_tensor", var_r, sq_r, 1.0 / D, tmp_r,
                       ALU.mult, ALU.subtract)
        C.act(tmp_r, var_r, AF.Ln, bias=tn["eps_col"][0:1, 0:1],
              waits=[("ve", ve_var)])
        se_rstd = C.act(rstd_r, tmp_r, AF.Exp, scale=-0.5)
        se_s2 = None
        if want_rms:
            ms_r = row_slot(6)
            C.vop("tensor_scalar_add", tmp_r, var_r, LN_EPS, waits=[("se", se_rstd)])
            C.vop("reciprocal", sum_r, tmp_r)
            C.vop("tensor_mul", ms_r, var_r, sum_r)
            ve_ms = C.vop("tensor_scalar_add", ms_r, ms_r, RMS_EPS)
            C.act(ms_r, ms_r, AF.Ln, waits=[("ve", ve_ms)])
            se_s2 = C.act(ms_r, ms_r, AF.Exp, scale=-0.5)
        del sum_r, sq_r
        C.mm(bank(2), ones_row(2), mean_r,
             waits=[("se", se_rstd)] + bar)
        te_b = C.mm(bank(3), ones_row(4), rstd_r)
        if want_rms:
            te_b = C.mm(bank(4), ones_row(6), row_slot(6),
                        waits=[("se", se_s2)])
        return te_b

    F_LN, F_TMP, F_H0, F_H1, F_SLAB = 0, 16384, 22528, 55296, 88064

    def load_wslab_a(wname, fs, buf_off, waits):
        tok = None
        for sp in range(4):
            src = tn[wname].ap().rearrange("(a p) f -> a p f", p=128)[
                4 * sp:4 * sp + 4, :, fs * 512:(fs + 1) * 512].rearrange(
                "a p f -> p a f")
            dst = sb_bf[:, buf_off // 2 + sp * 4 * 512:
                        buf_off // 2 + (sp + 1) * 4 * 512].rearrange(
                "p (a f) -> p a f", a=4)
            tok = C.dma("gq", dst, src, waits=waits if sp == 0 else ())
        return tok

    def load_wslab_b(wname, dt, buf_off, waits):
        tok = None
        for sp in range(4):
            src = tn[wname].ap().rearrange("(a p) d -> a p d", p=128)[
                8 * sp:8 * sp + 8, :, dt * 128:(dt + 1) * 128].rearrange(
                "a p d -> p a d")
            dst = sb_bf[:, buf_off // 2 + sp * 8 * 128:
                        buf_off // 2 + (sp + 1) * 8 * 128].rearrange(
                "p (a d) -> p a d", a=8)
            tok = C.dma("gq", dst, src, waits=waits if sp == 0 else ())
        return tok

    slab_guard = {"te": 0}
    wo_pref = {}

    def ffn_pass():
        ffn_pass.calls = getattr(ffn_pass, "calls", -1) + 1
        for ck in range(NCH):
            c0 = ck * CH
            ln_c = v_bf(F_LN, 16 * CH)
            h0c = v_bf(F_H0, 32 * CH)
            h1c = v_bf(F_H1, 32 * CH)
            t1 = v_f(F_TMP, CH)
            e0 = v_f(F_TMP + 2048, CH)
            e1 = v_f(F_TMP + 4096, CH)
            te_b = stats_ln(c0, F_H1, F_H0)
            ve_apply = []
            for dt in range(DT):
                xs = x_sb[:, dt * TPC + c0: dt * TPC + c0 + CH]
                C.vop("tensor_sub", t1, xs, bank(2), waits=[("te", te_b)])
                ve_apply.append(
                    C.vop("tensor_mul", ln_c[:, dt * CH:(dt + 1) * CH], t1, bank(3)))
            ve_ln = ve_apply[-1]
            for dt in range(DT):
                w = [("ve", ve_apply[dt]), ("gq", gq_consts)]
                C.mm(bank_row(0), tn["wg_sb"][:, 2 * dt:2 * dt + 1],
                     ln_c[:, dt * CH:(dt + 1) * CH], start=(dt == 0),
                     stop=(dt == DT - 1), waits=w, inc=(dt == DT - 1))
                C.mm(bank_row(1), tn["wg_sb"][:, 2 * dt + 1:2 * dt + 2],
                     ln_c[:, dt * CH:(dt + 1) * CH], start=(dt == 0),
                     stop=(dt == DT - 1), inc=(dt == DT - 1))
            te_g = C.counts["te"]
            g0_r, m_r = row_slot(0), row_slot(1)
            se_g = C.act(g0_r, bank_row(0), AF.Identity, waits=[("te", te_g)])
            C.vop("tensor_sub", m_r, g0_r, bank_row(1), waits=[("se", se_g)])
            ve_m = C.vop("tensor_single_scalar", m_r, m_r, 0.0, ALU.is_ge)
            te_mb = C.mm(bank(4), ones_row(8), m_r,
                         waits=[("ve", ve_m)])
            ve_m8 = C.vop("tensor_single_scalar", tn["mask_u8"][:, :], bank(4),
                          0.5, ALU.is_ge, waits=[("te", te_mb)])
            if DEBUG_TAP and ck == 0 and ffn_pass.calls == 0:
                C.dma("sync", tn["dbg"].ap()[0:1, 0:3584], rows_pool[0:1, :],
                      waits=[("ve", ve_m8)])
            SL = [F_SLAB, F_SLAB + 16384]
            ready = {}
            consumed = {}
            h_se = {}
            n_slab = 0
            for e in range(2):
                wname = "w0aT" if e == 0 else "w1aT"
                bcol = tn["b0a_col"] if e == 0 else tn["b1a_col"]
                func = AF.Gelu if e == 0 else AF.Relu
                hc = h0c if e == 0 else h1c
                for fs in range(8):
                    buf = n_slab % 2
                    if n_slab >= 2:
                        gw = [("te", consumed[n_slab - 2])]
                    else:
                        gw = [("te", slab_guard["te"])]
                    ready[n_slab] = load_wslab_a(wname, fs, SL[buf], gw)
                    slab = v_bf(SL[buf], 16 * CH)
                    for ft in range(4):
                        fg = fs * 4 + ft
                        bk = (n_slab * 4 + ft) % 2
                        w = [("gq", ready[n_slab]), ("ve", ve_m8)]
                        key = n_slab * 4 + ft
                        if key - 2 in h_se:
                            w.append(("se", h_se[key - 2]))
                        for kt in range(DT):
                            C.mm(bank(bk),
                                 slab[:, kt * 512 + ft * 128: kt * 512 + ft * 128 + 128],
                                 ln_c[:, kt * CH:(kt + 1) * CH],
                                 start=(kt == 0), stop=(kt == DT - 1),
                                 waits=w if kt == 0 else (), inc=(kt == DT - 1))
                        te_h = C.counts["te"]
                        h_se[key] = C.act(hc[:, fg * CH:(fg + 1) * CH], bank(bk),
                                          func, bias=bcol[:, fg:fg + 1],
                                          waits=[("te", te_h)])
                    consumed[n_slab] = C.counts["te"]
                    n_slab += 1
            h_done = (h_se[8 * 4 - 1], h_se[16 * 4 - 1])
            slab_guard["te"] = consumed[15]
            SLB = [F_SLAB, F_SLAB + 8192, F_SLAB + 16384, F_SLAB + 24576]
            bready = {}
            bconsumed = {}
            out_ve = []
            ev_se = []
            n_b = 0
            for dt in range(DT):
                for e in range(2):
                    wname = "w0bT" if e == 0 else "w1bT"
                    buf = n_b % 4
                    if n_b >= 4:
                        gw = [("te", bconsumed[n_b - 4])]
                    else:
                        gw = [("te", slab_guard["te"])]
                    bready[n_b] = load_wslab_b(wname, dt, SLB[buf], gw)
                    slabb = v_bf(SLB[buf], 32 * 128)
                    hc = h0c if e == 0 else h1c
                    bk = (dt % 2) * 2 + e
                    w = [("gq", bready[n_b]), ("se", h_done[e])]
                    if len(out_ve) >= 2:
                        w.append(("ve", out_ve[-2]))
                    for ft in range(FT):
                        C.mm(bank(bk), slabb[:, ft * 128:(ft + 1) * 128],
                             hc[:, ft * CH:(ft + 1) * CH],
                             start=(ft == 0), stop=(ft == FT - 1),
                             waits=w if ft == 0 else (), inc=(ft == FT - 1))
                    bconsumed[n_b] = C.counts["te"]
                    n_b += 1
                te_out = C.counts["te"]
                bk0, bk1 = (dt % 2) * 2, (dt % 2) * 2 + 1
                w = [("te", te_out)]
                if out_ve:
                    w.append(("ve", out_ve[-1]))
                C.act(e0, bank(bk0), AF.Identity, bias=tn["b0bh_col"][:, dt:dt + 1],
                      scale=0.5, waits=w)
                se_e = C.act(e1, bank(bk1), AF.Identity,
                             bias=tn["b1bh_col"][:, dt:dt + 1], scale=0.5)
                xs = x_sb[:, dt * TPC + c0: dt * TPC + c0 + CH]
                C.vop("select", t1, tn["mask_u8"][:, :], e0, e1,
                      waits=[("se", se_e), ("ve", ve_m8)])
                out_ve.append(C.vop("tensor_add", xs, xs, t1))
            slab_guard["te"] = bconsumed[n_b - 1]

    if _ph_on("ffn1"):
        ffn_pass()

    if _ph_on("proj"):
        P_XN, P_PBF, P_TMP, P_PSH, P_RT2, P_KRR = (
            0, 32768, 65536, 69632, 71680, 73728)
        kv_in = tn["kv_in"]
        xn = v_bf(P_XN, 16 * 1024)
        p_bf = v_bf(P_PBF, 16 * 1024)
        t1 = v_f(P_TMP, CH)
        t2 = v_f(P_TMP + 2048, CH)
        psh = v_bf(P_PSH, 1024)
        rt2 = v_bf(P_RT2, 1024)
        kr_ring = [v_bf(P_KRR + i * 2048, 1024) for i in range(2)]
        ve_ln2 = 0
        for ck in range(NCH):
            c0 = ck * CH
            te_b = stats_ln(c0, P_PBF, P_PBF + 16384, want_rms=True)
            for dt in range(DT):
                xs = x_sb[:, dt * TPC + c0: dt * TPC + c0 + CH]
                C.vop("tensor_sub", t1, xs, bank(2), waits=[("te", te_b)])
                C.vop("tensor_mul", t2, t1, bank(3))
                C.vop("tensor_add", xs, xs, t2)
                ve_ln2 = C.vop(
                    "scalar_tensor_tensor",
                    xn[:, dt * 1024 + c0: dt * 1024 + c0 + CH],
                    t2, tn["rms_col"][:, dt:dt + 1], bank(4),
                    ALU.mult, ALU.mult)
        SL = [F_SLAB, F_SLAB + 16384]
        ready = {}
        consumed = {}
        wq_se = {}
        ng = 0
        for fs in range(4):
            buf = fs % 2
            gw = ([("te", consumed[fs - 2])] if fs >= 2
                  else [("te", slab_guard["te"])])
            ready[fs] = load_wslab_a("wqT", fs, SL[buf], gw)
            slab = v_bf(SL[buf], 16 * CH)
            for ft in range(4):
                fg = fs * 4 + ft
                for ck in range(NCH):
                    c0 = ck * CH
                    bk = 4 + (ng % 2)
                    w = [("gq", ready[fs]), ("ve", ve_ln2)]
                    if ng - 2 in wq_se:
                        w.append(("se", wq_se[ng - 2]))
                    for kt in range(DT):
                        C.mm(bank(bk),
                             slab[:, kt * 512 + ft * 128: kt * 512 + ft * 128 + 128],
                             xn[:, kt * 1024 + c0: kt * 1024 + c0 + CH],
                             start=(kt == 0), stop=(kt == DT - 1),
                             waits=w if kt == 0 else (), inc=(kt == DT - 1))
                    te_w = C.counts["te"]
                    wq_se[ng] = C.act(
                        p_bf[:, fg * 1024 + c0: fg * 1024 + c0 + CH], bank(bk),
                        AF.Identity, bias=tn["bq_col"][:, fg:fg + 1],
                        waits=[("te", te_w)])
                    ng += 1
            consumed[fs] = C.counts["te"]
        slab_guard["te"] = consumed[3]
        se_wq_of = {}
        for h in range(H):
            se_wq_of[h] = wq_se[(h // 4) * 8 + (h % 4) * 2 + 1]
        se_wq = wq_se[ng - 1]
        cosb, sinb = tn["cos_bf"], tn["sin_bf"]
        kv_stage = {}
        for h in range(H):
            krh = kr_ring[h % 2]
            w = [("se", se_wq_of[h])]
            if h >= 2:
                w.append(("sq", kv_stage[h - 2]))
            C.vop("tensor_copy", psh[0:64, :],
                  p_bf[64:128, h * 1024:(h + 1) * 1024], waits=w)
            C.vop("tensor_copy", psh[64:128, :],
                  p_bf[0:64, h * 1024:(h + 1) * 1024])
            C.vop("tensor_mul", krh, p_bf[:, h * 1024:(h + 1) * 1024],
                  cosb[:, :])
            C.vop("tensor_mul", rt2, psh, sinb[:, :])
            ve_kr = C.vop("tensor_add", krh, krh, rt2)
            kv_stage[h] = C.dma(
                "sync", kv_in.ap()[128 * h: 128 * (h + 1), :], krh,
                waits=[("ve", ve_kr)])
            if h % 4 == 3:
                qn = h // 4
                C.coll("AllGather",
                       [kv_in.ap()[qn * 512:(qn + 1) * 512, :].opt()],
                       [tn["kr_gout"].ap()[qn * 4096:(qn + 1) * 4096, :].opt()],
                       waits=[("sq", kv_stage[h])])
        P_VST = 77824
        vstage = [v_bf(P_VST + i * 4096, 2048) for i in range(2)]
        kv_in_flat = kv_in.ap().rearrange("r c -> (r c)")
        vt_grp_se = {}
        vt_grp_te = {}
        dma_vst = {}
        v_staged = 0
        gidx = 0
        for tt in range(8):
            for sub in range(2):
                bko = (gidx % 2) * 1024
                for i in range(8):
                    h = sub * 8 + i
                    w = [("se", se_wq_of[h])]
                    if gidx >= 2 and i == 0:
                        w.append(("se", vt_grp_se[gidx - 2]))
                    C.transpose(
                        tn["psT"][:, bko + i * 128: bko + (i + 1) * 128],
                        p_bf[:, h * 1024 + tt * 128: h * 1024 + tt * 128 + 128],
                        waits=w, inc=(i == 7))
                vt_grp_te[gidx] = C.counts["te"]
                for i in range(8):
                    h = sub * 8 + i
                    w = [("te", vt_grp_te[gidx])] if i == 0 else []
                    if tt >= 2 and i == 0:
                        w.append(("sq", dma_vst[tt - 2]))
                    vt_grp_se[gidx] = C.act(
                        vstage[tt % 2][:, h * 128:(h + 1) * 128],
                        tn["psT"][:, bko + i * 128: bko + (i + 1) * 128],
                        AF.Copy, waits=w)
                gidx += 1
            dst = kv_in_flat.rearrange("(half t d) -> half t d",
                                       half=2, t=1024, d=2048)[
                1, tt * 128:(tt + 1) * 128, :]
            dma_vst[tt] = C.dma("sync", dst, vstage[tt % 2][:, :],
                                waits=[("se", vt_grp_se[gidx - 1])])
            v_staged = dma_vst[tt]
        C.coll("AllGather", [kv_in.ap()[2048:4096, :].opt()],
               [tn["v_gout"].ap().opt()], waits=[("sq", v_staged)])

    def attn_phase():
        A_KR, A_V, A_A, A_OST, A_SQ, A_RB = (
            0, 32768, 65536, 73728, 77824, 81920)
        kr_sb = v_bf(A_KR, 2 * S)
        v_sb = v_bf(A_V, 2 * S)
        a_ring = [v_bf(A_A + i * 2048, 1024) for i in range(2)]
        o_stage = [v_bf(A_OST + i * 1024, CH) for i in range(2)]
        sq_scr = v_bf(A_SQ, 2048)
        rb_sb = [v_f(A_RB + i * 2048, CH) for i in range(2)]
        o_in = tn["o_in"]
        bar = C.barrier()
        C.gwait("sync", bar)
        C.gwait("gpsimd", bar)
        if ATTN_UPTO == "gather":
            return
        gq_kr = 0
        for jh in range(2):
            dst = kr_sb[:, jh * S: (jh + 1) * S].rearrange("p (r t) -> p r t", r=8)
            gq_kr = C.gdma_off(
                dst, tn["kr_gout"], jh,
                [[1024, 128], [512 * 1024, 8], [1, 1024]],
                waits=[("cc", C.counts["cc"] - 1), ("sq", sq_consts)])
        for h in range(HPC):
            for cc_ in range(4):
                C.act(sq_scr, kr_sb[:, h * S + cc_ * 2048: h * S + (cc_ + 1) * 2048],
                      AF.Square,
                      waits=([("gq", gq_kr)] + bar) if (cc_ == 0 and h == 0) else ())
                se_sq = C.counts["se"]
                for k4 in range(4):
                    C.mm(bank_row(0), tn["ones128_bf"][:, :],
                         sq_scr[:, k4 * 512:(k4 + 1) * 512],
                         waits=[("se", se_sq), ("ve", C.counts["ve"])] + (
                             bar if (cc_ == 0 and h == 0 and k4 == 0) else []))
                    te_n = C.counts["te"]
                    C.vop("reduce_max",
                          row_slot(1)[0:1, 16 * h + 4 * cc_ + k4:
                                      16 * h + 4 * cc_ + k4 + 1],
                          bank_row(0), axis=AX.X, waits=[("te", te_n)])
        C.vop("reduce_max", row_slot(2)[0:1, 0:1], row_slot(1)[0:1, 0:32],
              axis=AX.X)
        ve_c = C.vop("tensor_scalar_mul", row_slot(2)[0:1, 0:1],
                     row_slot(2)[0:1, 0:1], -INV_SQRT_DK)
        te_c = C.mm(bank(0, 128, 1), ones_row(2),
                    row_slot(2)[0:1, 0:1], waits=[("ve", ve_c)])
        se_negc = C.act(tn["negC_col"][:, 0:1], bank(0, 128, 1), AF.Copy,
                        waits=[("te", te_c)])
        gq_kv = 0
        for jh in range(2):
            dst = v_sb[:, jh * S: (jh + 1) * S]
            gq_kv = C.gdma_off(
                dst, tn["v_gout"], 2 + jh,
                [[2048, 128], [128 * 2048, 64], [1, 128]],
                waits=[("cc", C.counts["cc"])] if jh == 0 else ())
        se_vt = 0
        if ATTN_UPTO in ("loads", "vt", "cbound"):
            C.gwait("sync", [("gq", gq_kv), ("se", se_negc)])
            return
        ve_precalc = C.counts["ve"]
        if ATTN_UPTO == "cbound":
            C.gwait("sync", [("se", C.counts["se"]), ("ve", ve_precalc)])
            return
        s_guard_se = [0, 0]
        a_guard_te = [0, 0]
        o_guard_ve = [0, 0]
        rs_guard_se = [0, 0]
        rs_row_guard_ve = [0, 0]
        rec_row_guard_te = [0, 0]
        ost_guard_sq = [0, 0]
        rb_guard_ve = [0, 0]
        pend_epi = [None]
        first_s = True
        for qc in range(16):
            E = 4 * qc + 4
            exp_toks = {}
            s_toks = {}
            av_w = {}

            def emit_avrs(kt):
                for h in range(HPC):
                    pair = kt % 2
                    w = list(av_w[kt]) if h == 0 else []
                    if kt == 0:
                        w += [("ve", o_guard_ve[h]), ("se", rs_guard_se[h])]
                    C.mm(bank(2 + h),
                         v_sb[:, (h * 64 + kt) * 128:(h * 64 + kt + 1) * 128],
                         a_ring[pair][:, h * CH:(h + 1) * CH],
                         start=(kt == 0), stop=(kt == E - 1), waits=w)
                    a_guard_te[pair] = C.mm(
                        bank_row(4 + h), tn["ones128_bf"][:, :],
                        a_ring[pair][:, h * CH:(h + 1) * CH],
                        start=(kt == 0), stop=(kt == E - 1))

            for kt in range(E):
                for h in range(HPC):
                    w = [("se", se_negc), ("se", s_guard_se[h])]
                    if first_s:
                        w += [("gq", gq_kv), ("ve", C.counts["ve"])]
                        first_s = False
                    s_toks[(kt, h)] = C.mm(
                        bank(h),
                        kr_sb[:, h * S + kt * 128: h * S + kt * 128 + 128],
                        kr_sb[:, h * S + qc * 512: h * S + qc * 512 + 512],
                        waits=w)
                if kt == 1 and pend_epi[0] is not None:
                    pend_epi[0]()
                    pend_epi[0] = None
                pair = kt % 2
                for h in range(HPC):
                    w = [("te", s_toks[(kt, h)]), ("te", a_guard_te[pair])]
                    exp_toks[(kt, h)] = C.act(
                        a_ring[pair][:, h * CH:(h + 1) * CH], bank(h), AF.Exp,
                        bias=tn["negC_col"][:, 0:1], scale=INV_SQRT_DK, waits=w)
                    s_guard_se[h] = exp_toks[(kt, h)]
                exp_toks[kt] = exp_toks[(kt, 1)]
                if kt >= E - 4:
                    j = kt - 4 * qc
                    for h in range(HPC):
                        ve_mask = C.vop(
                            "tensor_mul", a_ring[pair][:, h * CH:(h + 1) * CH],
                            a_ring[pair][:, h * CH:(h + 1) * CH],
                            tn["masks_sb"][:, j * 512:(j + 1) * 512],
                            waits=[("se", exp_toks[kt])] if h == 0 else ())
                    av_w[kt] = [("ve", ve_mask)]
                else:
                    av_w[kt] = [("se", exp_toks[kt])]
                if kt >= 1:
                    emit_avrs(kt - 1)
            emit_avrs(E - 1)
            te_avdone = C.counts["te"]
            ve_recs = [0, 0]
            for h in range(HPC):
                rs_r = row_slot(3 + h)
                rec_r = row_slot(5 + h)
                se_rs = C.act(rs_r, bank_row(4 + h), AF.Identity,
                              waits=[("te", te_avdone),
                                     ("ve", rs_row_guard_ve[h])])
                rs_row_guard_ve[h] = ve_recs[h] = C.vop(
                    "reciprocal", rec_r, rs_r,
                    waits=[("se", se_rs), ("te", rec_row_guard_te[h])])

            def make_epi(qc=qc, ve_recs=tuple(ve_recs)):
                def epi():
                    for h in range(HPC):
                        rec_r = row_slot(5 + h)
                        te_bc = C.mm(bank(4 + h), ones_row(5 + h), rec_r,
                                     waits=[("ve", ve_recs[h])])
                        rec_row_guard_te[h] = te_bc
                        se_cp = C.act(rb_sb[h], bank(4 + h), AF.Copy,
                                      waits=[("te", te_bc),
                                             ("ve", rb_guard_ve[h])])
                        rs_guard_se[h] = se_cp
                        ve_o = C.vop("tensor_mul", o_stage[h], bank(2 + h),
                                     rb_sb[h],
                                     waits=[("se", se_cp),
                                            ("sq", ost_guard_sq[h])])
                        o_guard_ve[h] = ve_o
                        rb_guard_ve[h] = ve_o
                        ost_guard_sq[h] = C.dma(
                            "sync",
                            o_in.ap()[128 * h: 128 * (h + 1),
                                      qc * 512: qc * 512 + 512],
                            o_stage[h], waits=[("ve", ve_o)])
                return epi

            pend_epi[0] = make_epi()
        if pend_epi[0] is not None:
            pend_epi[0]()
            pend_epi[0] = None
        o_dma_last = max(ost_guard_sq)
        if ATTN_UPTO == "main":
            C.gwait("sync", [("sq", o_dma_last)])
            return
        if _ph_on("wo"):
            wo_pref[0] = load_wslab_a("woT", 0, F_SLAB,
                                      [("te", slab_guard["te"])])
            wo_pref[1] = load_wslab_a("woT", 1, F_SLAB + 16384, [])
        C.coll("AllGather", [o_in.ap()[0:128, :].opt()],
               [tn["o_gout"].ap()[0:1024, :].opt()],
               waits=[("sq", ost_guard_sq[0])])
        C.coll("AllGather", [o_in.ap()[128:256, :].opt()],
               [tn["o_gout"].ap()[1024:2048, :].opt()],
               waits=[("sq", ost_guard_sq[1])])

    if _ph_on("attn"):
        attn_phase()

    if _ph_on("wo"):
        W_OFM, W_TMP = 0, 32768
        o_fm = v_bf(W_OFM, 16 * 1024)
        e0 = v_f(W_TMP, CH)
        bar = C.barrier()
        C.gwait("sync", bar)
        C.gwait("gpsimd", bar)
        sq_par = [0, 0]
        for par in range(2):
            dst = o_fm[:, :].rearrange("p (r two t) -> p two r t",
                                       two=2, t=1024)[:, par, :, :]
            sq_par[par] = C.gdma_off(
                dst, tn["o_gout"], 10 + par,
                [[8192, 128], [128 * 8192, 8], [1, 1024]],
                waits=[("cc", C.counts["cc"] - 1 + par)])
        SL = [F_SLAB, F_SLAB + 16384]
        consumed = {}
        wo_ve = []
        ng = 0
        for fs in range(4):
            buf = fs % 2
            if fs in wo_pref:
                ready = wo_pref[fs]
            else:
                gw = ([("te", consumed[fs - 2])] if fs >= 2
                      else [("te", slab_guard["te"])])
                ready = load_wslab_a("woT", fs, SL[buf], gw)
            slab = v_bf(SL[buf], 16 * CH)
            for ft in range(4):
                fg = fs * 4 + ft
                for ck in range(NCH):
                    c0 = ck * CH
                    bk = 4 + (ng % 2)
                    w = [("gq", ready), ("gq", sq_par[0])]
                    if ng == 0:
                        w += bar
                    if len(wo_ve) >= 2:
                        w.append(("ve", wo_ve[-2]))
                    kt_order = list(range(0, DT, 2)) + list(range(1, DT, 2))
                    for i_kt, kt in enumerate(kt_order):
                        if i_kt == DT // 2:
                            w = [("gq", sq_par[1])]
                        C.mm(bank(bk),
                             slab[:, kt * 512 + ft * 128: kt * 512 + ft * 128 + 128],
                             o_fm[:, kt * 1024 + c0: kt * 1024 + c0 + CH],
                             start=(i_kt == 0), stop=(i_kt == DT - 1),
                             waits=w if i_kt in (0, DT // 2) else (),
                             inc=(i_kt == DT - 1))
                    te_w = C.counts["te"]
                    wse = [("te", te_w)]
                    if wo_ve:
                        wse.append(("ve", wo_ve[-1]))
                    se_e = C.act(e0, bank(bk), AF.Identity,
                                 bias=tn["bo_col"][:, fg:fg + 1], waits=wse)
                    xs = x_sb[:, fg * TPC + c0: fg * TPC + c0 + CH]
                    wo_ve.append(C.vop("tensor_add", xs, xs, e0,
                                       waits=[("se", se_e)]))
                    ng += 1
            consumed[fs] = C.counts["te"]
        slab_guard["te"] = consumed[3]

    if _ph_on("ffn2"):
        ffn_pass()

    bar = C.barrier()
    st = 0
    for dt in range(DT):
        st = C.dma("sync", tn["out"].ap()[dt * 128:(dt + 1) * 128, :],
                   x_sb[:, dt * TPC:(dt + 1) * TPC], waits=bar if dt == 0 else ())
    C.gwait("sync", [("sq", st)])
    C.gwait("gpsimd", [("sq", st)])



def build():
    nc = bass.Bass("TRN2", target_bir_lowering=False)
    tn = {}

    tn["xT"] = nc.dram_tensor("xT", [D, TPC], F32, kind="ExternalInput").ap()
    tn["out"] = nc.dram_tensor("out_y", [D, TPC], F32, kind="ExternalOutput")
    tn["dbg"] = nc.dram_tensor("dbg", [128, 4096], F32, kind="ExternalOutput")
    tn["wqT"] = nc.dram_tensor("wqT", [D, D], F32, kind="ExternalInput")
    tn["woT"] = nc.dram_tensor("woT", [D, D], F32, kind="ExternalInput")
    tn["w0aT"] = nc.dram_tensor("w0aT", [D, FF], F32, kind="ExternalInput")
    tn["w0bT"] = nc.dram_tensor("w0bT", [FF, D], F32, kind="ExternalInput")
    tn["w1aT"] = nc.dram_tensor("w1aT", [D, FF], F32, kind="ExternalInput")
    tn["w1bT"] = nc.dram_tensor("w1bT", [FF, D], F32, kind="ExternalInput")
    tn["wg_cols"] = nc.dram_tensor("wg_cols", [128, 32], F32, kind="ExternalInput")
    for name, k in [("bq_col", 16), ("bo_col", 16), ("b0a_col", 32),
                    ("b1a_col", 32), ("b0b_col", 16), ("b1b_col", 16),
                    ("rms_col", 16)]:
        tn["dram_" + name] = nc.dram_tensor(name, [128, k], F32,
                                            kind="ExternalInput")
    tn["dram_pos_sb"] = nc.dram_tensor("pos", [1, TPC], F32, kind="ExternalInput")

    tn["dram_ones1_f"] = nc.inline_tensor(np.ones((1, 128), np.float32), "c_ones1")
    tn["dram_i64_sb"] = nc.inline_tensor(
        np.arange(64, dtype=np.float32).reshape(1, 64), "c_i64")
    tn["dram_eps_col"] = nc.inline_tensor(
        np.full((1, 1), LN_EPS, np.float32), "c_eps")
    tn["dram_ones128_bf"] = nc.inline_tensor(
        np.ones((128, 1), ml_dtypes.bfloat16), "c_ones128")
    tn["dram_eye_bf"] = nc.inline_tensor(
        np.eye(128, dtype=ml_dtypes.bfloat16), "c_eye")
    masks = np.zeros((128, 4 * 512), np.float32)
    for j in range(4):
        k_idx = np.arange(128)[:, None]
        q_idx = np.arange(512)[None, :]
        masks[:, j * 512:(j + 1) * 512] = (j * 128 + k_idx <= q_idx)
    tn["dram_masks"] = nc.inline_tensor(masks.astype(ml_dtypes.bfloat16), "c_masks")

    tn["kv_in"] = nc.dram_tensor("kv_in", [4096, 1024], BF16)
    tn["kr_gout"] = nc.dram_tensor("kr_gout", [8 * 2048, 1024], BF16,
                                   addr_space="Shared")
    tn["v_gout"] = nc.dram_tensor("v_gout", [8 * 2048, 1024], BF16,
                                  addr_space="Shared")
    tn["o_in"] = nc.dram_tensor("o_in", [256, 8192], BF16)
    tn["o_gout"] = nc.dram_tensor("o_gout", [2048, 8192], BF16,
                                  addr_space="Shared")
    tn["dram_meta"] = nc.dram_tensor("meta", [1, 64], mybir.dt.uint32,
                                     kind="ExternalInput")

    with ExitStack() as ctx:
        e = ctx.enter_context
        tn["x_sb"] = e(nc.sbuf_tensor("x_sb", [128, DT * TPC], F32))
        tn["masks_sb"] = e(nc.sbuf_tensor("masks_sb", [128, 2048], BF16))
        tn["ones128_bf"] = e(nc.sbuf_tensor("ones128_bf", [128, 1], BF16))
        tn["eye_sb"] = e(nc.sbuf_tensor("eye_sb", [128, 128], BF16))
        tn["ones1_f"] = e(nc.sbuf_tensor("ones1_f", [1, 128], F32))
        for name, k in [("bq_col", 16), ("bo_col", 16), ("b0a_col", 32),
                        ("b1a_col", 32), ("b0b_col", 16), ("b1b_col", 16),
                        ("rms_col", 16), ("b0bh_col", 16), ("b1bh_col", 16)]:
            tn[name] = e(nc.sbuf_tensor(name + "_sb", [128, k], F32))
        tn["wg_sb"] = e(nc.sbuf_tensor("wg_sb", [128, 32], BF16))
        tn["i64_sb"] = e(nc.sbuf_tensor("i64_sb", [1, 64], F32))
        tn["invf_sb"] = e(nc.sbuf_tensor("invf_sb", [1, 64], F32))
        tn["rows_pool"] = e(nc.sbuf_tensor("rows_pool", [1, 3584], F32))
        tn["cos_bf"] = e(nc.sbuf_tensor("cos_bf", [128, TPC], BF16))
        tn["sin_bf"] = e(nc.sbuf_tensor("sin_bf", [128, TPC], BF16))
        tn["negC_col"] = e(nc.sbuf_tensor("negC_col", [128, 2], F32))
        tn["mask_u8"] = e(nc.sbuf_tensor("mask_u8", [128, 512], U8))
        tn["eps_col"] = e(nc.sbuf_tensor("eps_col", [1, 1], F32))
        tn["meta_sb"] = e(nc.sbuf_tensor("meta_sb", [1, 64], mybir.dt.uint32))
        SCR = 120832
        tn["scratch"] = e(nc.sbuf_tensor("scratch", [128, SCR], U8))
        tn["psA"] = e(nc.psum_tensor("psA", [128, 3072], F32))
        tn["psT"] = e(nc.psum_tensor("psT", [128, 2048], BF16))

        sems = {}
        for s_ in ENGS:
            sems[s_] = e(nc.semaphore("sem_" + s_))
        block = e(nc.Block())

        def run(eng_name):
            def f(eng):
                C = Ctx(nc, eng_name, eng, sems, tn)
                if eng_name == "gpsimd":
                    with eng.register("roff") as roff:
                        C.roff = roff
                        emit_program(C)
                else:
                    emit_program(C)
            return f

        block.gpsimd(run("gpsimd"))
        block.scalar(run("scalar"))
        block.tensor(run("tensor"))
        block.vector(run("vector"))
        block.sync(run("sync"))

    return nc


_CACHED = {}


def kernel(**inputs):
    X = np.asarray(inputs["L0_attn_X"])
    asc = np.ascontiguousarray

    def colpack(v, k):
        return asc(np.asarray(v, np.float32).reshape(k, 128).T)

    shared = {
        "wqT": asc(np.asarray(inputs["L0_attn_Wq_w"], np.float32).T),
        "woT": asc(np.asarray(inputs["L0_attn_Wo_w"], np.float32).T),
        "w0aT": asc(np.asarray(inputs["L0_ffn_W0a"], np.float32).T),
        "w0bT": asc(np.asarray(inputs["L0_ffn_W0b"], np.float32).T),
        "w1aT": asc(np.asarray(inputs["L0_ffn_W1a"], np.float32).T),
        "w1bT": asc(np.asarray(inputs["L0_ffn_W1b"], np.float32).T),
        "wg_cols": asc(np.asarray(inputs["L0_ffn_Wg"], np.float32).T
                       .reshape(16, 128, 2).transpose(1, 0, 2).reshape(128, 32)),
        "bq_col": colpack(inputs["L0_attn_Wq_b"], 16),
        "bo_col": colpack(inputs["L0_attn_Wo_b"], 16),
        "b0a_col": colpack(inputs["L0_ffn_b0a"], 32),
        "b1a_col": colpack(inputs["L0_ffn_b1a"], 32),
        "b0b_col": colpack(inputs["L0_ffn_b0b"], 16),
        "b1b_col": colpack(inputs["L0_ffn_b1b"], 16),
        "rms_col": colpack(inputs["L0_attn_rms_w"], 16),
    }
    in_maps = []
    for c in range(NCORE):
        m = dict(shared)
        m["xT"] = asc(X[0, c * TPC:(c + 1) * TPC, :].T)
        m["pos"] = np.arange(c * TPC, (c + 1) * TPC,
                             dtype=np.float32).reshape(1, TPC)
        meta = np.zeros(64, np.uint32)
        for jh in range(2):
            meta[jh] = ((c // 2) * 4096 + ((2 * c) % 4 + jh) * 128) * 1024
            meta[2 + jh] = 256 * c + 128 * jh
        for gh in range(2):
            meta[10 + gh] = 1024 * c + gh * 8 * 128 * 8192
        m["meta"] = meta.reshape(1, 64)
        in_maps.append(m)

    if "nc" not in _CACHED:
        _CACHED["nc"] = build()
    res = run_bass_kernel_spmd(_CACHED["nc"], in_maps, core_ids=list(range(NCORE)))
    y = np.empty((S, D), np.float32)
    for c in range(NCORE):
        y[c * TPC:(c + 1) * TPC, :] = res.results[c]["out_y"].T
    return y.reshape(1, S, D)
